# version 06
# baseline (speedup 1.0000x reference)
import numpy as np
import ml_dtypes
from contextlib import ExitStack

import concourse.bass as bass
import concourse.mybir as mybir
from concourse.bass_utils import run_bass_kernel_spmd

F32 = mybir.dt.float32
BF16 = mybir.dt.bfloat16
U8 = mybir.dt.uint8
AF = mybir.ActivationFunctionType
ALU = mybir.AluOpType

B, S, D = 4, 4096, 2048
E = 2 * D
NH, P, N = 64, 64, 64
KCONV = 4
CONV_DIM = E + 2 * N
PROJ = E + CONV_DIM + NH
HID = 5632
TOK = 2048
CS = 128
NCH = TOK // CS
G = 512
NG = TOK // G
KD = D // 128
KH = HID // 128
NBLK = 32
LN_EPS = 1e-5
RMS_EPS = 1e-6
FCONSTS = (1e-5, 1e-6, 1.0, -0.5, -1.0, 0.5, 0.0, 2.0)

ENGINES = ("sync", "tensor", "vector", "scalar", "gpsimd")
SEMS = ("qs", "qw", "dv", "ac", "pe", "cc")


class Env:
    def __init__(self, nc, ctx):
        self.nc = nc
        self.ctx = ctx
        self.sems = {}
        self.cnt = {}
        self.streams = {e: [] for e in ENGINES}

    def _sem(self, name):
        if name not in self.sems:
            self.sems[name] = self.ctx.enter_context(self.nc.semaphore(name))
            self.cnt[name] = 0
        return self.sems[name]

    def op(self, engine, fn, waits=(), incs=()):
        for s, v in waits:
            self._sem(s)
            assert v <= self.cnt[s], f"wait on future milestone {s}:{v} (now {self.cnt[s]})"
        ms = {}
        for sname, k in incs:
            self._sem(sname)
            self.cnt[sname] += k
            ms[sname] = self.cnt[sname]
        self.streams[engine].append((tuple(waits), fn, tuple(incs)))
        return ms

    def wait(self, engine, waits):
        self.op(engine, None, waits=waits)

    def now(self, *names):
        return [(n, self.cnt.get(n, 0)) for n in names]

    def barrier(self):
        b = [(n, self.cnt[n]) for n in self.sems]
        for e in ENGINES:
            self.wait(e, b)

    def replay(self, block):
        for ename in ENGINES:
            stream = self.streams[ename]
            if not stream:
                continue

            def emit(eng, stream=stream):
                seen = {}
                for waits, fn, incs in stream:
                    for sname, val in waits:
                        if seen.get(sname, -1) < val:
                            eng.wait_ge(self.sems[sname], val)
                            seen[sname] = val
                    if fn is None:
                        continue
                    inst = fn(eng)
                    for sname, k in incs:
                        inst.then_inc(self.sems[sname], k)

            getattr(block, ename)(emit)


def build_program(dbg=(), upto=4):
    nc = bass.Bass("TRN2", target_bir_lowering=False)
    dram = lambda name, shape, dtype, kind="Internal": nc.dram_tensor(name, shape, dtype, kind=kind)

    x_in = dram("x_in", [KD, 128, TOK], F32, kind="ExternalInput")
    wiT = dram("wiT", [128, KD, PROJ], F32, kind="ExternalInput")
    woT = dram("woT", [32, 128, KD, 128], F32, kind="ExternalInput")
    wg8 = dram("wg8", [KH, 128, KD, 128], U8, kind="ExternalInput")
    wu8 = dram("wu8", [KH, 128, KD, 128], U8, kind="ExternalInput")
    wd8 = dram("wd8", [KH, 128, KD, 128], U8, kind="ExternalInput")
    consts = dram("consts", [6, 128, 128], F32, kind="ExternalInput")
    convw_in = dram("convw", [128, 33 * KCONV], F32, kind="ExternalInput")
    convb_in = dram("convb", [128, 33], F32, kind="ExternalInput")
    dtb_in = dram("dtb", [64, 1], F32, kind="ExternalInput")
    alog_in = dram("alog", [64, 1], F32, kind="ExternalInput")
    drow_in = dram("drow", [1, 64], F32, kind="ExternalInput")
    nw_in = dram("nw", [128, 32], F32, kind="ExternalInput")
    sg_in = dram("sg", [128, KH], F32, kind="ExternalInput")
    su_in = dram("su", [128, KH], F32, kind="ExternalInput")
    sd_in = dram("sd", [128, KD], F32, kind="ExternalInput")
    masks_in = dram("masks", [128, 2], F32, kind="ExternalInput")
    fconst_in = dram("fconst", [128, 8], F32, kind="ExternalInput")

    out_d = dram("out", [KD, 128, TOK], F32, kind="ExternalOutput")

    def spill(name, shape, dtype):
        return dram(name, shape, dtype, kind=("ExternalOutput" if name in dbg else "Internal"))

    x1_sp = spill("x1_sp", [KD, 128, TOK], F32)
    x2_sp = spill("x2_sp", [KD, 128, TOK], F32)
    y_sp = spill("y_sp", [NBLK, 128, TOK], BF16)
    gate_sp = spill("gate_sp", [NBLK, 128, TOK], BF16)
    dbg_bc = spill("dbg_bc", [128, TOK], BF16) if "dbg_bc" in dbg else None
    dbg_dt = spill("dbg_dt", [64, TOK], BF16) if "dbg_dt" in dbg else None
    dbg_st = spill("dbg_st", [64, NH * P], F32) if "dbg_st" in dbg else None
    dbg_halo = spill("dbg_halo", [128, KD * 3], F32) if "dbg_halo" in dbg else None
    dbg_xc = spill("dbg_xc", [128, KD, TOK], BF16) if "dbg_xc" in dbg else None

    halo_snd = dram("halo_snd", [128, KD * 3], F32)
    halo_rcv = dram("halo_rcv", [128, KD * 3], F32)
    st_snd = dram("st_snd", [64, NH * P], F32)
    flag_snd = dram("flag_snd", [1, 8], F32)
    flag_rcv = dram("flag_rcv", [1, 8], F32)
    st_rcv = dram("st_rcv", [64, NH * P], F32)

    with ExitStack() as ctx:
        env = Env(nc, ctx)
        sb = lambda name, shape, dtype: ctx.enter_context(nc.sbuf_tensor(name, shape, dtype))
        pb = [ctx.enter_context(nc.psum_tensor(f"pb{i}", [128, 512], F32)) for i in range(8)]
        pbb = [t.ap().bitcast(BF16) for t in pb]

        eye_bf = sb("eye_bf", [128, 128], BF16)
        tri_f = sb("tri_f", [128, 128], F32)
        tri_bf = sb("tri_bf", [128, 128], BF16)
        ut_bf = sb("ut_bf", [128, 128], BF16)
        ones_bf = sb("ones_bf", [128, 128], BF16)
        neg128_bf = sb("neg128_bf", [1, 128], BF16)
        ones_f = sb("ones_f", [65, 128], F32)

        def ones_row(row_ap):
            bp = row_ap.base_partition()
            return ones_f[bp:bp + 1, :]
        convw = sb("convw_sb", [128, 33 * KCONV], F32)
        convb = sb("convb_sb", [128, 33], F32)
        dtb = sb("dtb_sb", [64, 1], F32)
        a_bc = sb("a_bc", [128, 64], BF16)
        d_bc = sb("d_bc", [128, 64], BF16)
        nw = sb("nw_sb", [128, 32], F32)
        sg = sb("sg_sb", [128, KH], F32)
        su = sb("su_sb", [128, KH], F32)
        sdh = sb("sdh_sb", [128, KD], F32)
        masks = sb("masks_sb", [128, 2], F32)
        d_row_sb = sb("d_row_sb", [1, 64], F32)
        a_row_f = sb("a_row_f", [1, 64], F32)
        fconst = sb("fconst_sb", [128, 8], F32)
        ach_row = sb("ach_row", [1, 64], F32)
        ea_row = sb("ea_row", [1, 64], F32)
        runa_row = sb("runa_row", [1, 64], F32)
        hrow = sb("hrow", [1, 16], F32)
        rms_row = sb("rms_row", [1, 512], F32)
        sum_bf = sb("sum_bf", [1, 2 * G], BF16)
        rowsA = sb("rowsA", [65, TOK], F32)
        rowsB = sb("rowsB", [65, TOK], F32)
        rowsS = sb("rowsS", [65, 4 * G], F32)
        mu_t_row = rowsA.ap()[0:1, :]
        rstd_t_row = rowsA.ap()[32:33, :]
        mu2_row = rowsA.ap()[64:65, :]
        rstd2_row = rowsB.ap()[0:1, :]
        mu3_row = rowsB.ap()[32:33, :]
        rstd3_row = rowsB.ap()[64:65, :]
        stat_row = rowsS.ap()[0:1, :]

        cst = consts.ap()
        env.op("gpsimd", lambda e: e.dma_start(out=eye_bf[:], in_=cst[0]), incs=[("qw", 16)])
        env.op("sync", lambda e: e.dma_start(out=tri_f[:], in_=cst[1]), incs=[("qs", 16)])
        env.op("gpsimd", lambda e: e.dma_start(out=tri_bf[:], in_=cst[1]), incs=[("qw", 16)])
        env.op("gpsimd", lambda e: e.dma_start(out=ut_bf[:], in_=cst[2]), incs=[("qw", 16)])
        env.op("gpsimd", lambda e: e.dma_start(out=ones_bf[:], in_=cst[3]), incs=[("qw", 16)])
        env.op("gpsimd", lambda e: e.dma_start(out=neg128_bf[:], in_=cst[4][0:1, :]), incs=[("qw", 16)])
        env.op("sync", lambda e: e.dma_start(out=ones_f[:], in_=cst[3][0:65, :]), incs=[("qs", 16)])
        env.op("sync", lambda e: e.dma_start(out=convw[:], in_=convw_in.ap()), incs=[("qs", 16)])
        env.op("sync", lambda e: e.dma_start(out=convb[:], in_=convb_in.ap()), incs=[("qs", 16)])
        env.op("sync", lambda e: e.dma_start(out=dtb[:], in_=dtb_in.ap()), incs=[("qs", 16)])
        env.op("sync", lambda e: e.dma_start(out=nw[:], in_=nw_in.ap()), incs=[("qs", 16)])
        env.op("sync", lambda e: e.dma_start(out=sg[:], in_=sg_in.ap()), incs=[("qs", 16)])
        env.op("sync", lambda e: e.dma_start(out=su[:], in_=su_in.ap()), incs=[("qs", 16)])
        env.op("sync", lambda e: e.dma_start(out=sdh[:], in_=sd_in.ap()), incs=[("qs", 16)])
        env.op("sync", lambda e: e.dma_start(out=masks[:], in_=masks_in.ap()), incs=[("qs", 16)])
        env.op("sync", lambda e: e.dma_start(out=d_row_sb[:], in_=drow_in.ap()), incs=[("qs", 16)])
        env.op("sync", lambda e: e.dma_start(out=a_row_f[:], in_=alog_in.ap().rearrange("p one -> one p")), incs=[("qs", 16)])
        env.op("sync", lambda e: e.dma_start(out=fconst[:], in_=fconst_in.ap()), incs=[("qs", 16)])
        for i, v in enumerate(FCONSTS):
            nc.const_aps.register(F32, float(v), fconst[:, i:i + 1])
        w0 = env.now("qs", "qw")
        env.op("vector", lambda e: e.tensor_scalar(out=convw[:], in0=convw[:], scalar1=0.5, scalar2=None, op0=ALU.mult), waits=w0, incs=[("dv", 1)])
        env.op("vector", lambda e: e.tensor_scalar(out=convb[:], in0=convb[:], scalar1=0.5, scalar2=None, op0=ALU.mult), incs=[("dv", 1)])
        env.op("vector", lambda e: e.tensor_scalar(out=sdh[:], in0=sdh[:], scalar1=0.5, scalar2=None, op0=ALU.mult), incs=[("dv", 1)])
        env.op("scalar", lambda e: e.activation(out=a_row_f[:], in_=a_row_f[:], func=AF.Exp), waits=w0, incs=[("ac", 1)])
        env.op("vector", lambda e: e.tensor_scalar(out=a_row_f[:], in0=a_row_f[:], scalar1=-1.0, scalar2=None, op0=ALU.mult), waits=env.now("ac"), incs=[("dv", 1)])
        env.op("vector", lambda e: e.memset(runa_row[:], 0.0), incs=[("dv", 1)])
        env.op("tensor", lambda e: e.matmul(out=pb[6][:, 0:64], lhsT=ones_f[0:1, :], rhs=a_row_f[:], start=True, stop=True), waits=env.now("dv", "qs"), incs=[("pe", 1)])
        env.op("tensor", lambda e: e.matmul(out=pb[6][:, 64:128], lhsT=ones_f[0:1, :], rhs=d_row_sb[:], start=True, stop=True), incs=[("pe", 1)])
        env.op("vector", lambda e: e.tensor_copy(out=a_bc[:], in_=pb[6][:, 0:64]), waits=env.now("pe"), incs=[("dv", 1)])
        env.op("vector", lambda e: e.tensor_copy(out=d_bc[:], in_=pb[6][:, 64:128]), incs=[("dv", 1)])
        prep_done = env.now("dv", "ac", "pe", "qs", "qw")

        def ln_stats(src, g, mu_row, rstd_row, eps, xb, xb_extra=()):
            srcf = src.ap().rearrange("p k g -> p (k g)")
            xbf = xb.ap().rearrange("p k g -> p (k g)")
            mdv = env.op("vector", lambda e: e.tensor_copy(out=xbf, in_=srcf), incs=[("dv", 1)], waits=list(xb_extra))
            for k in range(KD):
                env.op("tensor", lambda e, k=k: e.matmul(out=pb[6][0:1, 0:G], lhsT=ones_bf[:, 0:1], rhs=xb[:, k, :], start=(k == 0), stop=(k == KD - 1)),
                       waits=([("dv", mdv["dv"])] + env.now("ac") if k == 0 else []), incs=[("pe", 1)])
            sxpe = env.cnt["pe"]
            e1 = env.op("scalar", lambda e: e.copy(out=stat_row[:, 0:G], in_=pb[6][0:1, 0:G]), waits=[("pe", sxpe)], incs=[("ac", 1)])
            mac = env.op("scalar", lambda e: e.activation(out=xbf, in_=srcf, func=AF.Square), waits=[("pe", sxpe)] + list(xb_extra), incs=[("ac", 1)])
            for k in range(KD):
                env.op("tensor", lambda e, k=k: e.matmul(out=pb[6][0:1, 0:G], lhsT=ones_bf[:, 0:1], rhs=xb[:, k, :], start=(k == 0), stop=(k == KD - 1)),
                       waits=([("ac", mac["ac"])] if k == 0 else []), incs=[("pe", 1)])
            e2 = env.op("scalar", lambda e: e.copy(out=stat_row[:, G:2 * G], in_=pb[6][0:1, 0:G]), waits=env.now("pe"), incs=[("ac", 1)])
            env.op("vector", lambda e, g=g: e.tensor_scalar(out=mu_row[:, g * G:(g + 1) * G], in0=stat_row[:, 0:G], scalar1=1.0 / D, scalar2=None, op0=ALU.mult),
                   waits=[("ac", e1["ac"])], incs=[("dv", 1)])
            mmu = env.cnt["dv"]
            env.op("scalar", lambda e, g=g: e.activation(out=stat_row[:, 2 * G:3 * G], in_=mu_row[:, g * G:(g + 1) * G], func=AF.Square),
                   waits=[("dv", mmu)], incs=[("ac", 1)])
            env.op("vector", lambda e: e.tensor_scalar(out=stat_row[:, G:2 * G], in0=stat_row[:, G:2 * G], scalar1=1.0 / D, scalar2=None, op0=ALU.mult),
                   waits=[("ac", e2["ac"])], incs=[("dv", 1)])
            env.op("vector", lambda e: e.tensor_sub(out=stat_row[:, 3 * G:4 * G], in0=stat_row[:, G:2 * G], in1=stat_row[:, 2 * G:3 * G]),
                   waits=env.now("ac"), incs=[("dv", 1)])
            env.op("scalar", lambda e: e.activation(out=stat_row[:, 3 * G:4 * G], in_=stat_row[:, 3 * G:4 * G], func=AF.Ln, bias=float(eps)),
                   waits=env.now("dv"), incs=[("ac", 1)])
            env.op("scalar", lambda e, g=g: e.activation(out=rstd_row[:, g * G:(g + 1) * G], in_=stat_row[:, 3 * G:4 * G], func=AF.Exp, scale=-0.5),
                   incs=[("ac", 1)])
            return {"mu_dv": mmu, "rstd_ac": env.cnt["ac"], "xb_done": env.now("pe", "ac")}

        _uniq = [0]

        def emit_ffn(src_dram, dst_dram, stats, next_ln, first_wait=(), do_halo=False):
            _uniq[0] += 1
            _u = _uniq[0]
            with ExitStack() as fs:
                fsb = lambda name, shape, dtype: fs.enter_context(nc.sbuf_tensor(f"{name}_u{_u}", shape, dtype))
                xg = fsb("xg", [128, KD, G], F32)
                xb = fsb("xb", [128, KD, G], BF16)
                x1g = fsb("x1g", [128, KD, G], F32)
                hid = fsb("hid", [128, KH, G], BF16)
                g_pre = fsb("g_pre", [128, G], BF16)
                g_act = fsb("g_act", [128, G], BF16)
                u_pre = fsb("u_pre", [128, G], BF16)
                rstd_bc_sb = fsb("rstd_bc_sb", [128, G], F32)
                halo_stg = fsb("halo_stg", [128, KD * 3], F32) if do_halo else None
                wgt = [fsb(f"wgt{i}", [128, KD, 128], BF16) for i in range(3)]
                wut = [fsb(f"wut{i}", [128, KD, 128], BF16) for i in range(3)]
                wdt = [fsb(f"wdt{i}", [128, 4, 128], BF16) for i in range(6)]

                slot_free = {0: [], 1: [], 2: []}
                dn_slot_free = {i: [] for i in range(6)}
                x1g_free = []
                hid_free_pe = 0
                for g in range(NG):
                    wx = list(first_wait) if g == 0 else env.now("dv", "ac")
                    mx = env.op("sync", lambda e, g=g: e.dma_start(out=xg.ap(),
                                                                   in_=src_dram.ap()[:, :, g * G:(g + 1) * G].rearrange("k p g -> p k g")),
                                waits=wx, incs=[("qx0", 16)])
                    xw = [("qx0", mx["qx0"])]
                    if stats is None:
                        st = ln_stats(xg, g, mu_t_row, rstd_t_row, LN_EPS, xb, xb_extra=xw)
                        mu_row, rstd_row = mu_t_row, rstd_t_row
                        rstd_w = [("ac", st["rstd_ac"])]
                    else:
                        mu_row, rstd_row = stats
                        st = {"mu_dv": env.cnt["dv"]}
                        rstd_w = []
                    mb = env.op("tensor", lambda e, g=g, mu_row=mu_row: e.matmul(out=pb[7][:, 0:G], lhsT=ones_row(mu_row), rhs=mu_row[:, g * G:(g + 1) * G], start=True, stop=True),
                                waits=[("dv", st["mu_dv"])] + env.now("dv") + (list(first_wait) if g == 0 and stats is not None else []), incs=[("pe", 1)])
                    env.op("vector", lambda e: e.tensor_tensor(out=xb.ap(), in0=xg.ap(),
                                                               in1=pb[7][:, 0:G].unsqueeze(1).broadcast_to((128, KD, G)),
                                                               op=ALU.subtract),
                           waits=[("pe", mb["pe"])] + xw + env.now("ac"), incs=[("dv", 1)])
                    xc_dv = env.cnt["dv"]
                    mb2 = env.op("tensor", lambda e, g=g, rstd_row=rstd_row: e.matmul(out=pb[7][:, 0:G], lhsT=ones_row(rstd_row), rhs=rstd_row[:, g * G:(g + 1) * G], start=True, stop=True),
                                 waits=[("dv", xc_dv)] + rstd_w, incs=[("pe", 1)])
                    env.op("vector", lambda e: e.tensor_copy(out=rstd_bc_sb[:], in_=pb[7][:, 0:G]), waits=[("pe", mb2["pe"])], incs=[("dv", 1)])
                    rbc_dv = env.cnt["dv"]
                    for k in range(KD):
                        env.op("tensor", lambda e, k=k: e.matmul(out=pb[6][0:1, 0:G], lhsT=ones_bf[:, 0:1], rhs=xb[:, k, :], start=(k == 0), stop=(k == KD - 1)),
                               waits=([("dv", xc_dv)] + env.now("ac") if k == 0 else []), incs=[("pe", 1)])
                    env.op("scalar", lambda e: e.copy(out=sum_bf[:, 0:G], in_=pb[6][0:1, 0:G]), waits=env.now("pe"), incs=[("ac", 1)])
                    sx_ac = env.cnt["ac"]

                    prev_gate_dv = {0: 0, 1: 0}
                    prev_up_dv = {0: 0, 1: rbc_dv}
                    for kh in range(KH):
                        slot = kh % 3
                        par = kh % 2
                        bg = 0 if par == 0 else 6
                        bu = 1 if par == 0 else 7
                        mwg = env.op("gpsimd", lambda e, kh=kh, slot=slot: e.dma_start(out=wgt[slot].ap().rearrange("p k h -> p (k h)"),
                                                                                      in_=wg8.ap()[kh].rearrange("p k h -> p (k h)")),
                                     waits=slot_free[slot], incs=[(f"qa{slot}", 16)])
                        mwu = env.op("gpsimd", lambda e, kh=kh, slot=slot: e.dma_start(out=wut[slot].ap().rearrange("p k h -> p (k h)"),
                                                                                      in_=wu8.ap()[kh].rearrange("p k h -> p (k h)")),
                                     incs=[(f"qb{slot}", 16)])
                        for k in range(KD):
                            env.op("tensor", lambda e, k=k, slot=slot, bg=bg: e.matmul(out=pb[bg][:, 0:G], lhsT=wgt[slot][:, k, :], rhs=xb[:, k, :],
                                                                                start=(k == 0), stop=False),
                                   waits=([(f"qa{slot}", mwg[f"qa{slot}"]), ("ac", sx_ac), ("dv", prev_gate_dv[par])] if k == 0 else []), incs=[("pe", 1)])
                        env.op("tensor", lambda e, bg=bg: e.matmul(out=pb[bg][:, 0:G], lhsT=neg128_bf[:], rhs=sum_bf[:, 0:G], start=False, stop=True), incs=[("pe", 1)])
                        gmm = env.cnt["pe"]
                        for k in range(KD):
                            env.op("tensor", lambda e, k=k, slot=slot, bu=bu: e.matmul(out=pb[bu][:, 0:G], lhsT=wut[slot][:, k, :], rhs=xb[:, k, :],
                                                                                start=(k == 0), stop=False),
                                   waits=([(f"qb{slot}", mwu[f"qb{slot}"]), ("dv", prev_up_dv[par])] if k == 0 else []), incs=[("pe", 1)])
                        env.op("tensor", lambda e, bu=bu: e.matmul(out=pb[bu][:, 0:G], lhsT=neg128_bf[:], rhs=sum_bf[:, 0:G], start=False, stop=True), incs=[("pe", 1)])
                        umm = env.cnt["pe"]
                        slot_free[slot] = [("pe", umm)]
                        env.op("vector", lambda e, kh=kh, bg=bg: e.scalar_tensor_tensor(out=g_pre[:], in0=pb[bg][:, 0:G], scalar=sg[:, kh:kh + 1],
                                                                                 in1=rstd_bc_sb[:], op0=ALU.mult, op1=ALU.mult),
                               waits=[("pe", gmm), ("dv", rbc_dv)], incs=[("dv", 1)])
                        prev_gate_dv[par] = env.cnt["dv"]
                        env.op("scalar", lambda e: e.activation(out=g_act[:], in_=g_pre[:], func=AF.Silu), waits=env.now("dv"), incs=[("ac", 1)])
                        env.op("vector", lambda e, kh=kh, bu=bu: e.scalar_tensor_tensor(out=u_pre[:], in0=pb[bu][:, 0:G], scalar=su[:, kh:kh + 1],
                                                                                 in1=rstd_bc_sb[:], op0=ALU.mult, op1=ALU.mult),
                               waits=[("pe", umm)], incs=[("dv", 1)])
                        prev_up_dv[par] = env.cnt["dv"]
                        env.op("vector", lambda e, kh=kh: e.tensor_mul(out=hid[:, kh, :], in0=g_act[:], in1=u_pre[:]),
                               waits=env.now("ac") + ([("pe", hid_free_pe)] if kh == 0 else []), incs=[("dv", 1)])
                    hid_dv = env.cnt["dv"]
                    for kh in range(KH):
                        env.op("tensor", lambda e, kh=kh: e.matmul(out=pb[6][0:1, 0:G], lhsT=ones_bf[:, 0:1], rhs=hid[:, kh, :],
                                                                   start=(kh == 0), stop=(kh == KH - 1)),
                               waits=([("dv", hid_dv)] + env.now("ac") if kh == 0 else []), incs=[("pe", 1)])
                    env.op("scalar", lambda e: e.copy(out=sum_bf[:, G:2 * G], in_=pb[6][0:1, 0:G]), waits=env.now("pe"), incs=[("ac", 1)])
                    sh_ac = env.cnt["ac"]
                    for mg in range(4):
                        for kh in range(KH):
                            slot = kh % 6
                            mwd = env.op("gpsimd", lambda e, kh=kh, mg=mg, slot=slot: e.dma_start(out=wdt[slot].ap().rearrange("p a b -> p (a b)"),
                                                                                                 in_=wd8.ap()[kh][:, mg * 4:(mg + 1) * 4, :].rearrange("p a b -> p (a b)")),
                                         waits=dn_slot_free.get(slot, []), incs=[(f"qd{slot}", 16)])
                            for mt in range(4):
                                env.op("tensor", lambda e, kh=kh, mt=mt, slot=slot: e.matmul(out=pb[2 + mt][:, 0:G], lhsT=wdt[slot][:, mt, :], rhs=hid[:, kh, :],
                                                                                             start=(kh == 0), stop=False),
                                       waits=([(f"qd{slot}", mwd[f"qd{slot}"])] + ([("dv", env.cnt["dv"])] if kh == 0 else []) if mt == 0 else []), incs=[("pe", 1)])
                            dn_slot_free[slot] = [("pe", env.cnt["pe"])]
                        for mt in range(4):
                            env.op("tensor", lambda e, mt=mt: e.matmul(out=pb[2 + mt][:, 0:G], lhsT=neg128_bf[:], rhs=sum_bf[:, G:2 * G], start=False, stop=True),
                                   waits=([("ac", sh_ac)] if mt == 0 else []), incs=[("pe", 1)])
                        dpe = env.cnt["pe"]
                        for mt in range(4):
                            kd = mg * 4 + mt
                            env.op("vector", lambda e, mt=mt, kd=kd: e.scalar_tensor_tensor(out=x1g[:, kd, :], in0=pb[2 + mt][:, 0:G], scalar=sdh[:, kd:kd + 1],
                                                                                           in1=xg[:, kd, :], op0=ALU.mult, op1=ALU.add),
                                   waits=[("pe", dpe)] + (x1g_free if (mg == 0 and mt == 0) else []), incs=[("dv", 1)])
                    hid_free_pe = env.cnt["pe"]
                    x1_dv = env.cnt["dv"]
                    mst = env.op("sync", lambda e, g=g: e.dma_start(out=dst_dram.ap()[:, :, g * G:(g + 1) * G].rearrange("k p g -> p k g"),
                                                                    in_=x1g.ap()),
                                 waits=[("dv", x1_dv)], incs=[("qst", 16)])
                    if next_ln is not None:
                        nmu, nrstd, neps = next_ln
                        ln_stats(x1g, g, nmu, nrstd, neps, xb, xb_extra=[("dv", x1_dv)])
                    x1g_free = [("qst", mst["qst"])] + env.now("dv", "ac")
                    if do_halo and g == NG - 1:
                        env.op("vector", lambda e: e.tensor_scalar(out=halo_stg.ap().rearrange("p (k t) -> p k t", t=3),
                                                                   in0=x1g.ap()[:, :, G - 3:G],
                                                                   scalar1=masks[:, 1:2], scalar2=None, op0=ALU.mult),
                               waits=[("dv", x1_dv)], incs=[("dv", 1)])
                        mhs = env.op("sync", lambda e: e.dma_start(out=halo_snd.ap(), in_=halo_stg[:]), waits=env.now("dv"), incs=[("qh", 16)])
                        env.wait("gpsimd", [("qh", mhs["qh"])])
                        env.op("gpsimd", lambda e: e.collective_compute("AllReduce", ALU.add,
                                                                        replica_groups=[[0, 1], [2, 3], [4, 5], [6, 7]],
                                                                        ins=[halo_snd.ap().opt()], outs=[halo_rcv.ap().opt()]),
                               incs=[("cc", 1)])
                        env.wait("gpsimd", env.now("cc"))
                        env.op("gpsimd", lambda e: e.collective_compute("AllReduce", ALU.add,
                                                                        replica_groups=[[0, 1], [2, 3], [4, 5], [6, 7]],
                                                                        ins=[flag_snd.ap().opt()], outs=[flag_rcv.ap().opt()]),
                               incs=[("cc", 1)])
                env.barrier()

        p23 = ExitStack()
        psb = lambda name, shape, dtype: p23.enter_context(nc.sbuf_tensor(name, shape, dtype))
        b_conv = psb("b_conv", [64, TOK], BF16)
        c_conv = psb("c_conv", [64, TOK], BF16)
        lam_all = psb("lam_all", [128, NCH * 64], F32)

        def emit_p2():
            with ExitStack() as ps:
                s2 = lambda name, shape, dtype: ps.enter_context(nc.sbuf_tensor(name, shape, dtype))
                xc_full = s2("xc_full", [128, KD, TOK], BF16)
                rstd2_bc = s2("rstd2_bc", [128, TOK], F32)
                wit = [s2(f"wit{i}", [128, KD, 128], BF16) for i in range(2)]
                rawb = [s2(f"rawb{i}", [128, 3 + TOK], BF16) for i in range(2)]
                hconv = [s2(f"hconv{i}", [128, TOK], BF16) for i in range(2)]
                cacc = s2("cacc", [128, TOK], F32)
                tanh_s = s2("tanh_s", [128, TOK], BF16)
                dtp_fm = s2("dtp_fm", [64, TOK], BF16)
                dtv = s2("dtv", [64, G], F32)
                dtv2 = s2("dtv2", [64, G], F32)
                dtv3 = s2("dtv3", [64, G], F32)
                gstg = [s2(f"gstg{i}", [128, G], BF16) for i in range(2)]
                ystg = [s2(f"ystg{i}", [128, 1024], BF16) for i in range(2)]
                dtp_tm = s2("dtp_tm", [128, NCH * 64], BF16)
                ad_tm = s2("ad_tm", [128, NCH * 64], BF16)
                dec_all = s2("dec_all", [128, NCH * 64], BF16)
                ea_all = s2("ea_all", [64, NCH * 64], BF16)
                btm_all = s2("btm_all", [128, NCH * 64], BF16)
                gm_all = s2("gm_all", [128, NCH * 128], BF16)
                acs_sb = s2("acs_sb", [128, 64], F32)
                scr64 = s2("scr64", [128, 64], F32)
                hct = [s2(f"hct{i}", [128, 128], BF16) for i in range(2)]
                dh = [s2(f"dh{i}", [128, 128], BF16) for i in range(2)]
                segr = [s2(f"segr{i}", [128, 256], BF16) for i in range(2)]
                mt_sb = [s2(f"mt_sb{i}", [128, 256], BF16) for i in range(2)]
                ms_sb = [s2(f"ms_sb{i}", [64, 256], BF16) for i in range(2)]
                bdec = [s2(f"bdec{i}", [64, 128], BF16) if False else s2(f"bdec{i}", [128, 128], BF16) for i in range(2)]
                st_cur = s2("st_cur", [64, 128], BF16)
                st_tmp = s2("st_tmp", [64, 128], BF16)
                ytm = s2("ytm", [128, 128], BF16)
                st_stage = s2("st_stage", [64, 128], F32)
                halo_sbuf = s2("halo_sbuf", [128, KD * 3], F32)
                hb_bf = s2("hb_bf", [128, KD * 3], BF16)
                xch = s2("xch", [128, KD * 3], BF16)
                rstdh_bc = s2("rstdh_bc", [128, 3], F32)

                x1k = [s2(f"x1k{i}", [128, G], F32) for i in range(2)]
                x1k_free = {0: [], 1: []}
                for g in range(NG):
                    mb = env.op("tensor", lambda e, g=g: e.matmul(out=pb[7][:, 0:G], lhsT=ones_row(mu2_row), rhs=mu2_row[:, g * G:(g + 1) * G], start=True, stop=True),
                                waits=env.now("dv"), incs=[("pe", 1)])
                    for k in range(KD):
                        ks = k % 2
                        mx = env.op("sync", lambda e, g=g, k=k, ks=ks: e.dma_start(out=x1k[ks][:], in_=x1_sp.ap()[k, :, g * G:(g + 1) * G]),
                                    waits=x1k_free[ks], incs=[(f"qx{ks}", 16)])
                        env.op("vector", lambda e, g=g, k=k, ks=ks: e.tensor_tensor(out=xc_full[:, k, g * G:(g + 1) * G],
                                                                                    in0=x1k[ks][:], in1=pb[7][:, 0:G], op=ALU.subtract),
                               waits=[("pe", mb["pe"]), (f"qx{ks}", mx[f"qx{ks}"])], incs=[("dv", 1)])
                        x1k_free[ks] = [("dv", env.cnt["dv"])]
                    mb2 = env.op("tensor", lambda e, g=g: e.matmul(out=pb[6][:, 0:G], lhsT=ones_row(rstd2_row), rhs=rstd2_row[:, g * G:(g + 1) * G], start=True, stop=True),
                                 waits=env.now("dv"), incs=[("pe", 1)])
                    env.op("vector", lambda e, g=g: e.tensor_copy(out=rstd2_bc[:, g * G:(g + 1) * G], in_=pb[6][:, 0:G]),
                           waits=[("pe", mb2["pe"])], incs=[("dv", 1)])
                mhr = env.op("sync", lambda e: e.dma_start(out=halo_sbuf[:], in_=halo_rcv.ap()), waits=env.now("cc"), incs=[("qh", 16)])
                env.op("vector", lambda e: e.tensor_scalar(out=hb_bf[:], in0=halo_sbuf[:], scalar1=masks[:, 0:1], scalar2=None, op0=ALU.mult),
                       waits=[("qh", mhr["qh"])], incs=[("dv", 1)])
                hdv = env.cnt["dv"]
                hb3 = hb_bf.ap().rearrange("p (k t) -> p k t", t=3)
                for k in range(KD):
                    env.op("tensor", lambda e, k=k: e.matmul(out=pb[6][0:1, 0:3], lhsT=ones_bf[:, 0:1], rhs=hb3[:, k, :], start=(k == 0), stop=(k == KD - 1)),
                           waits=([("dv", hdv)] if k == 0 else []), incs=[("pe", 1)])
                e1 = env.op("scalar", lambda e: e.copy(out=hrow[:, 0:3], in_=pb[6][0:1, 0:3]), waits=env.now("pe"), incs=[("ac", 1)])
                mq = env.op("scalar", lambda e: e.activation(out=hb_bf[:], in_=hb_bf[:], func=AF.Square), waits=env.now("pe"), incs=[("ac", 1)])
                for k in range(KD):
                    env.op("tensor", lambda e, k=k: e.matmul(out=pb[6][0:1, 0:3], lhsT=ones_bf[:, 0:1], rhs=hb3[:, k, :], start=(k == 0), stop=(k == KD - 1)),
                           waits=([("ac", mq["ac"])] if k == 0 else []), incs=[("pe", 1)])
                env.op("scalar", lambda e: e.copy(out=hrow[:, 3:6], in_=pb[6][0:1, 0:3]), waits=env.now("pe"), incs=[("ac", 1)])
                env.op("vector", lambda e: e.tensor_scalar(out=hrow[:, 0:3], in0=hrow[:, 0:3], scalar1=1.0 / D, scalar2=None, op0=ALU.mult),
                       waits=env.now("ac"), incs=[("dv", 1)])
                env.op("vector", lambda e: e.tensor_scalar(out=hrow[:, 3:6], in0=hrow[:, 3:6], scalar1=1.0 / D, scalar2=None, op0=ALU.mult), incs=[("dv", 1)])
                env.op("scalar", lambda e: e.activation(out=hrow[:, 6:9], in_=hrow[:, 0:3], func=AF.Square), waits=env.now("dv"), incs=[("ac", 1)])
                env.op("vector", lambda e: e.tensor_sub(out=hrow[:, 9:12], in0=hrow[:, 3:6], in1=hrow[:, 6:9]), waits=env.now("ac"), incs=[("dv", 1)])
                env.op("scalar", lambda e: e.activation(out=hrow[:, 9:12], in_=hrow[:, 9:12], func=AF.Ln, bias=LN_EPS), waits=env.now("dv"), incs=[("ac", 1)])
                env.op("scalar", lambda e: e.activation(out=hrow[:, 12:15], in_=hrow[:, 9:12], func=AF.Exp, scale=-0.5), incs=[("ac", 1)])
                mbh = env.op("tensor", lambda e: e.matmul(out=pb[7][:, 0:3], lhsT=ones_f[0:1, :], rhs=hrow[:, 0:3], start=True, stop=True),
                             waits=env.now("ac", "dv"), incs=[("pe", 1)])
                env.op("vector", lambda e: e.scalar_tensor_tensor(out=xch.ap().rearrange("p (k t) -> p k t", t=3),
                                                                  in0=halo_sbuf.ap().rearrange("p (k t) -> p k t", t=3),
                                                                  scalar=masks[:, 0:1],
                                                                  in1=pb[7][:, 0:3].unsqueeze(1).broadcast_to((128, KD, 3)),
                                                                  op0=ALU.mult, op1=ALU.subtract),
                       waits=[("pe", mbh["pe"])], incs=[("dv", 1)])
                mbh2 = env.op("tensor", lambda e: e.matmul(out=pb[7][:, 0:3], lhsT=ones_f[0:1, :], rhs=hrow[:, 12:15], start=True, stop=True),
                              waits=env.now("dv"), incs=[("pe", 1)])
                env.op("vector", lambda e: e.tensor_copy(out=rstdh_bc[:], in_=pb[7][:, 0:3]), waits=[("pe", mbh2["pe"])], incs=[("dv", 1)])
                pre_dv = env.cnt["dv"]
                if dbg_halo is not None:
                    env.op("gpsimd", lambda e: e.dma_start(out=dbg_halo.ap()[:, 0:45], in_=xch[:, 0:45]), waits=env.now("dv"), incs=[("qdbg", 16)])
                    env.op("gpsimd", lambda e: e.dma_start(out=dbg_halo.ap()[:, 45:48], in_=rstdh_bc[:]), incs=[("qdbg", 16)])
                if dbg_xc is not None:
                    env.op("sync", lambda e: e.dma_start(out=dbg_xc.ap(), in_=xc_full.ap()), waits=env.now("dv"), incs=[("qdbg", 16)])

                wit_free = {0: [], 1: []}
                blk_psum_free = {0: pre_dv, 1: pre_dv, 2: pre_dv, 3: pre_dv, 4: pre_dv}

                def inproj_block(c0, w, bi, ragged):
                    slot = bi % 2
                    mw = env.op("gpsimd", lambda e, c0=c0, w=w, slot=slot: e.dma_start(out=wit[slot].ap()[:, :, 0:w],
                                                                                      in_=wiT.ap()[:, :, c0:c0 + w]),
                                waits=wit_free[slot], incs=[(f"qi{slot}", 16)])
                    for k in range(KD):
                        for g in range(NG):
                            env.op("tensor", lambda e, k=k, g=g, slot=slot, w=w: e.matmul(out=pb[g][0:w, 0:G], lhsT=wit[slot][:, k, 0:w],
                                                                                          rhs=xc_full[:, k, g * G:(g + 1) * G],
                                                                                          start=(k == 0), stop=(k == KD - 1)),
                                   waits=(([(f"qi{slot}", mw[f"qi{slot}"])] if g == 0 else []) + [("dv", blk_psum_free[g])] if k == 0 else []), incs=[("pe", 1)])
                        if ragged:
                            env.op("tensor", lambda e, k=k, slot=slot, w=w: e.matmul(out=pb[4][0:w, 0:3], lhsT=wit[slot][:, k, 0:w],
                                                                                     rhs=xch.ap().rearrange("p (kk t) -> p kk t", t=3)[:, k, :],
                                                                                     start=(k == 0), stop=(k == KD - 1)),
                                   waits=([("dv", pre_dv), ("dv", blk_psum_free[4])] if k == 0 else []), incs=[("pe", 1)])
                    wit_free[slot] = [("pe", env.cnt["pe"])]
                    return env.cnt["pe"]

                def conv_block(src_raw, dst_bf, jconv, raw_dv, dst_free=()):
                    env.op("vector", lambda e, j=jconv: e.tensor_scalar(out=cacc[:], in0=src_raw[:, 0:TOK], scalar1=convw[:, j * 4:j * 4 + 1],
                                                                        scalar2=convb[:, j:j + 1], op0=ALU.mult, op1=ALU.add),
                           waits=[("dv", raw_dv)], incs=[("dv", 1)])
                    for tap in range(1, 4):
                        env.op("vector", lambda e, j=jconv, tap=tap: e.scalar_tensor_tensor(out=cacc[:], in0=src_raw[:, tap:tap + TOK],
                                                                                            scalar=convw[:, j * 4 + tap:j * 4 + tap + 1],
                                                                                            in1=cacc[:], op0=ALU.mult, op1=ALU.add),
                               incs=[("dv", 1)])
                    cdv0 = env.cnt["dv"]
                    env.op("scalar", lambda e: e.activation(out=tanh_s[:], in_=cacc[:], func=AF.Tanh), waits=[("dv", cdv0)], incs=[("ac", 1)])
                    env.op("vector", lambda e: e.scalar_tensor_tensor(out=dst_bf[:], in0=tanh_s[:], scalar=1.0, in1=cacc[:],
                                                                      op0=ALU.add, op1=ALU.mult),
                           waits=env.now("ac") + list(dst_free), incs=[("dv", 1)])
                    return env.cnt["dv"]

                dpe = inproj_block(PROJ - 64, 64, 0, ragged=False)
                for g in range(NG):
                    env.op("vector", lambda e, g=g: e.tensor_tensor(out=dtv[:], in0=pb[g][0:64, 0:G], in1=rstd2_bc[0:64, g * G:(g + 1) * G], op=ALU.mult),
                           waits=[("pe", dpe)], incs=[("dv", 1)])
                    vdv = env.cnt["dv"]
                    env.op("scalar", lambda e: e.activation(out=dtv2[:], in_=dtv[:], func=AF.Relu, bias=dtb[:]), waits=[("dv", vdv)], incs=[("ac", 1)])
                    env.op("scalar", lambda e: e.activation(out=dtv3[:], in_=dtv[:], func=AF.Abs, bias=dtb[:]), incs=[("ac", 1)])
                    env.op("scalar", lambda e: e.activation(out=dtv[:], in_=dtv3[:], func=AF.Exp, scale=-1.0), incs=[("ac", 1)])
                    env.op("scalar", lambda e: e.activation(out=dtv3[:], in_=dtv[:], func=AF.Ln, bias=1.0), incs=[("ac", 1)])
                    env.op("vector", lambda e, g=g: e.tensor_add(out=dtp_fm[:, g * G:(g + 1) * G], in0=dtv2[:], in1=dtv3[:]),
                           waits=env.now("ac"), incs=[("dv", 1)])
                for g in range(NG):
                    blk_psum_free[g] = env.cnt["dv"]
                if dbg_dt is not None:
                    env.op("sync", lambda e: e.dma_start(out=dbg_dt.ap(), in_=dtp_fm[:]), waits=env.now("dv"), incs=[("qdbg", 16)])

                bpe = inproj_block(E + E, 128, 1, ragged=True)
                for g in range(NG):
                    env.op("vector", lambda e, g=g: e.tensor_tensor(out=rawb[0][:, 3 + g * G:3 + (g + 1) * G], in0=pb[g][:, 0:G],
                                                                    in1=rstd2_bc[:, g * G:(g + 1) * G], op=ALU.mult),
                           waits=[("pe", bpe)], incs=[("dv", 1)])
                env.op("vector", lambda e: e.tensor_tensor(out=rawb[0][:, 0:3], in0=pb[4][:, 0:3], in1=rstdh_bc[:], op=ALU.mult),
                       waits=[("pe", bpe)], incs=[("dv", 1)])
                rawdv = env.cnt["dv"]
                env.op("vector", lambda e: e.tensor_scalar(out=cacc[:], in0=rawb[0][:, 0:TOK], scalar1=convw[:, 32 * 4:32 * 4 + 1],
                                                           scalar2=convb[:, 32:33], op0=ALU.mult, op1=ALU.add),
                       waits=[("dv", rawdv)], incs=[("dv", 1)])
                for tap in range(1, 4):
                    env.op("vector", lambda e, tap=tap: e.scalar_tensor_tensor(out=cacc[:], in0=rawb[0][:, tap:tap + TOK],
                                                                               scalar=convw[:, 32 * 4 + tap:32 * 4 + tap + 1],
                                                                               in1=cacc[:], op0=ALU.mult, op1=ALU.add),
                           incs=[("dv", 1)])
                env.op("scalar", lambda e: e.activation(out=tanh_s[:], in_=cacc[:], func=AF.Tanh), waits=env.now("dv"), incs=[("ac", 1)])
                env.op("vector", lambda e: e.scalar_tensor_tensor(out=b_conv[:], in0=tanh_s[0:64, 0:TOK], scalar=1.0, in1=cacc[0:64, 0:TOK],
                                                                  op0=ALU.add, op1=ALU.mult),
                       waits=env.now("ac"), incs=[("dv", 1)])
                env.op("vector", lambda e: e.scalar_tensor_tensor(out=c_conv[:], in0=tanh_s[64:128, 0:TOK], scalar=1.0, in1=cacc[64:128, 0:TOK],
                                                                  op0=ALU.add, op1=ALU.mult),
                       incs=[("dv", 1)])
                bc_dv = env.cnt["dv"]
                for g in range(NG):
                    blk_psum_free[g] = env.cnt["dv"]
                blk_psum_free[4] = env.cnt["dv"]
                if dbg_bc is not None:
                    env.op("sync", lambda e: e.dma_start(out=dbg_bc.ap()[0:64, :], in_=b_conv[:]), waits=env.now("dv"), incs=[("qdbg", 16)])
                    env.op("sync", lambda e: e.dma_start(out=dbg_bc.ap()[64:128, :], in_=c_conv[:]), incs=[("qdbg", 16)])

                pp5_free = env.cnt["dv"]
                pb6_free = env.cnt["dv"]
                for c in range(NCH):
                    cc0 = c * 64
                    m1 = env.op("tensor", lambda e, c=c: e.transpose(out=pbb[5][:, 0:64], in_=dtp_fm[0:64, c * CS:(c + 1) * CS], identity=eye_bf[0:64, 0:64]),
                                waits=env.now("dv") + [("dv", pp5_free)], incs=[("pe", 1)])
                    env.op("vector", lambda e, cc0=cc0: e.tensor_copy(out=dtp_tm[:, cc0:cc0 + 64], in_=pbb[5][:, 0:64]),
                           waits=[("pe", m1["pe"])], incs=[("dv", 1)])
                    env.op("vector", lambda e, cc0=cc0: e.tensor_tensor(out=ad_tm[:, cc0:cc0 + 64], in0=pbb[5][:, 0:64], in1=a_bc[:], op=ALU.mult),
                           incs=[("dv", 1)])
                    ad_dv = env.cnt["dv"]
                    m2 = env.op("tensor", lambda e, cc0=cc0: e.matmul(out=pb[6][:, 0:64], lhsT=tri_bf[:], rhs=ad_tm[:, cc0:cc0 + 64], start=True, stop=True),
                                waits=[("dv", ad_dv), ("dv", pb6_free)], incs=[("pe", 1)])
                    m3 = env.op("tensor", lambda e, cc0=cc0: e.matmul(out=pb[6][0:1, 64:128], lhsT=ones_bf[:, 0:1], rhs=ad_tm[:, cc0:cc0 + 64], start=True, stop=True),
                                incs=[("pe", 1)])
                    env.op("vector", lambda e: e.tensor_copy(out=acs_sb[:, 0:64], in_=pb[6][:, 0:64]), waits=[("pe", m3["pe"])], incs=[("dv", 1)])
                    env.op("vector", lambda e: e.tensor_copy(out=ach_row[:], in_=pb[6][0:1, 64:128]), incs=[("dv", 1)])
                    acs_dv = env.cnt["dv"]
                    m4 = env.op("tensor", lambda e, c=c: e.transpose(out=pbb[5][:, 0:64], in_=b_conv[:, c * CS:(c + 1) * CS], identity=eye_bf[0:64, 0:64]),
                                waits=[("dv", acs_dv), ("dv", bc_dv)], incs=[("pe", 1)])
                    m5 = env.op("tensor", lambda e, c=c: e.matmul(out=pb[5][:, 64:192], lhsT=b_conv[:, c * CS:(c + 1) * CS],
                                                                  rhs=c_conv[:, c * CS:(c + 1) * CS], start=True, stop=True),
                                incs=[("pe", 1)])
                    env.op("vector", lambda e, cc0=cc0: e.tensor_copy(out=btm_all[:, cc0:cc0 + 64], in_=pbb[5][:, 0:64]),
                           waits=[("pe", m5["pe"])], incs=[("dv", 1)])
                    env.op("vector", lambda e, c=c: e.tensor_tensor(out=gm_all[:, c * 128:(c + 1) * 128], in0=pb[5][:, 64:192], in1=tri_f[:], op=ALU.mult),
                           incs=[("dv", 1)])
                    pp5_free = env.cnt["dv"]
                    env.op("scalar", lambda e: e.activation(out=ea_row[:], in_=ach_row[:], func=AF.Exp), waits=[("dv", acs_dv)], incs=[("ac", 1)])
                    ea_ac = env.cnt["ac"]
                    env.op("tensor", lambda e: e.matmul(out=pb[6][:, 128:192], lhsT=ones_f[0:1, :], rhs=runa_row[:, 0:64], start=True, stop=True),
                           waits=[("dv", acs_dv)], incs=[("pe", 1)])
                    env.op("tensor", lambda e: e.matmul(out=pb[6][:, 192:256], lhsT=ones_f[0:1, :], rhs=ach_row[:, 0:64], start=True, stop=True),
                           incs=[("pe", 1)])
                    m8 = env.op("tensor", lambda e: e.matmul(out=pb[6][0:64, 256:320], lhsT=ones_f[0:1, 0:64], rhs=ea_row[:, 0:64], start=True, stop=True),
                               waits=[("ac", ea_ac)], incs=[("pe", 1)])
                    env.op("vector", lambda e: e.tensor_add(out=scr64[:], in0=pb[6][:, 128:192], in1=acs_sb[:, 0:64]),
                           waits=[("pe", m8["pe"])], incs=[("dv", 1)])
                    env.op("scalar", lambda e, cc0=cc0: e.activation(out=lam_all[:, cc0:cc0 + 64], in_=scr64[:], func=AF.Exp),
                           waits=env.now("dv"), incs=[("ac", 1)])
                    env.op("vector", lambda e: e.tensor_sub(out=scr64[:], in0=pb[6][:, 192:256], in1=acs_sb[:, 0:64]),
                           waits=env.now("ac"), incs=[("dv", 1)])
                    env.op("scalar", lambda e, cc0=cc0: e.activation(out=dec_all[:, cc0:cc0 + 64], in_=scr64[:], func=AF.Exp),
                           waits=env.now("dv"), incs=[("ac", 1)])
                    env.op("vector", lambda e, cc0=cc0: e.tensor_copy(out=ea_all[:, cc0:cc0 + 64], in_=pb[6][0:64, 256:320]), incs=[("dv", 1)])
                    pb6_free = env.cnt["dv"]
                    env.op("vector", lambda e: e.tensor_add(out=runa_row[:, 0:64], in0=runa_row[:, 0:64], in1=ach_row[:, 0:64]),
                           waits=env.now("ac"), incs=[("dv", 1)])

                seg_free = env.cnt["dv"]
                y_free = env.cnt["dv"]
                b7_free = env.cnt["dv"]
                gstg_free = {0: [], 1: []}
                ystg_free = {0: [], 1: []}
                raw_free = {0: [("dv", bc_dv)], 1: []}
                hconv_free = {0: 0, 1: 0}
                ststg_free = []
                conv_dv = {}

                def gen_p2a(j):
                    slot = j % 2
                    mw = env.op("gpsimd", lambda e, j=j, slot=slot: e.dma_start(out=wit[slot].ap()[:, :, 0:128],
                                                                               in_=wiT.ap()[:, :, j * 128:(j + 1) * 128]),
                                waits=wit_free[slot], incs=[(f"qi{slot}", 16)])
                    yield
                    for k in range(KD):
                        for g in range(NG):
                            env.op("tensor", lambda e, k=k, g=g, slot=slot: e.matmul(out=pb[g][:, 0:G], lhsT=wit[slot][:, k, :],
                                                                                     rhs=xc_full[:, k, g * G:(g + 1) * G],
                                                                                     start=(k == 0), stop=(k == KD - 1)),
                                   waits=(([(f"qi{slot}", mw[f"qi{slot}"])] if g == 0 else []) + [("dv", blk_psum_free[g])] if k == 0 else []),
                                   incs=[("pe", 1)])
                        yield
                    gpe = env.cnt["pe"]
                    wit_free[slot] = [("pe", gpe)]
                    for g in range(NG):
                        env.op("vector", lambda e, g=g, slot=slot: e.scalar_tensor_tensor(out=gstg[slot][:], in0=pb[g][:, 0:G], scalar=0.5,
                                                                                          in1=rstd2_bc[:, g * G:(g + 1) * G], op0=ALU.mult, op1=ALU.mult),
                               waits=[("pe", gpe)] + (gstg_free[slot] if g == 0 else env.now(f"qg{slot}")), incs=[("dv", 1)])
                        env.op("scalar", lambda e, slot=slot: e.activation(out=tanh_s[:, 0:G], in_=gstg[slot][:], func=AF.Tanh),
                               waits=env.now("dv"), incs=[("ac", 1)])
                        env.op("vector", lambda e, slot=slot: e.scalar_tensor_tensor(out=gstg[slot][:], in0=tanh_s[:, 0:G], scalar=1.0,
                                                                                     in1=gstg[slot][:], op0=ALU.add, op1=ALU.mult),
                               waits=env.now("ac"), incs=[("dv", 1)])
                        env.op("sync", lambda e, g=g, j=j, slot=slot: e.dma_start(out=gate_sp.ap()[j][:, g * G:(g + 1) * G], in_=gstg[slot][:]),
                               waits=env.now("dv"), incs=[(f"qg{slot}", 16)])
                        blk_psum_free[g] = env.cnt["dv"]
                        yield
                    gstg_free[slot] = [(f"qg{slot}", env.cnt[f"qg{slot}"])]
                    mwh = env.op("gpsimd", lambda e, j=j, slot=slot: e.dma_start(out=wit[slot].ap()[:, :, 0:128],
                                                                                in_=wiT.ap()[:, :, E + j * 128:E + (j + 1) * 128]),
                                 waits=wit_free[slot], incs=[(f"qi{slot}", 16)])
                    yield
                    for k in range(KD):
                        for g in range(NG):
                            env.op("tensor", lambda e, k=k, g=g, slot=slot: e.matmul(out=pb[g][:, 0:G], lhsT=wit[slot][:, k, :],
                                                                                     rhs=xc_full[:, k, g * G:(g + 1) * G],
                                                                                     start=(k == 0), stop=(k == KD - 1)),
                                   waits=(([(f"qi{slot}", mwh[f"qi{slot}"])] if g == 0 else []) + [("dv", blk_psum_free[g])] if k == 0 else []),
                                   incs=[("pe", 1)])
                        env.op("tensor", lambda e, k=k, slot=slot: e.matmul(out=pb[4][:, 0:3], lhsT=wit[slot][:, k, :],
                                                                            rhs=xch.ap().rearrange("p (kk t) -> p kk t", t=3)[:, k, :],
                                                                            start=(k == 0), stop=(k == KD - 1)),
                               waits=([("dv", pre_dv), ("dv", blk_psum_free[4])] if k == 0 else []), incs=[("pe", 1)])
                        yield
                    hpe = env.cnt["pe"]
                    wit_free[slot] = [("pe", hpe)]
                    for g in range(NG):
                        env.op("vector", lambda e, g=g, slot=slot: e.tensor_tensor(out=rawb[slot][:, 3 + g * G:3 + (g + 1) * G], in0=pb[g][:, 0:G],
                                                                                   in1=rstd2_bc[:, g * G:(g + 1) * G], op=ALU.mult),
                               waits=[("pe", hpe)] + (raw_free[slot] if g == 0 else []), incs=[("dv", 1)])
                        yield
                    env.op("vector", lambda e, slot=slot: e.tensor_tensor(out=rawb[slot][:, 0:3], in0=pb[4][:, 0:3], in1=rstdh_bc[:], op=ALU.mult),
                           incs=[("dv", 1)])
                    for g in range(NG):
                        blk_psum_free[g] = env.cnt["dv"]
                    blk_psum_free[4] = env.cnt["dv"]
                    rdv = env.cnt["dv"]
                    env.op("vector", lambda e, j=j, slot=slot: e.tensor_scalar(out=cacc[:], in0=rawb[slot][:, 0:TOK], scalar1=convw[:, j * 4:j * 4 + 1],
                                                                               scalar2=convb[:, j:j + 1], op0=ALU.mult, op1=ALU.add),
                           waits=[("dv", rdv)], incs=[("dv", 1)])
                    yield
                    for tap in range(1, 4):
                        env.op("vector", lambda e, j=j, tap=tap, slot=slot: e.scalar_tensor_tensor(out=cacc[:], in0=rawb[slot][:, tap:tap + TOK],
                                                                                                   scalar=convw[:, j * 4 + tap:j * 4 + tap + 1],
                                                                                                   in1=cacc[:], op0=ALU.mult, op1=ALU.add),
                               incs=[("dv", 1)])
                        yield
                    cdv0 = env.cnt["dv"]
                    env.op("scalar", lambda e: e.activation(out=tanh_s[:], in_=cacc[:], func=AF.Tanh), waits=[("dv", cdv0)], incs=[("ac", 1)])
                    env.op("vector", lambda e, slot=slot: e.scalar_tensor_tensor(out=hconv[slot][:], in0=tanh_s[:], scalar=1.0, in1=cacc[:],
                                                                                 op0=ALU.add, op1=ALU.mult),
                           waits=env.now("ac") + [("pe", hconv_free[slot])], incs=[("dv", 1)])
                    conv_dv[j] = env.cnt["dv"]
                    raw_free[slot] = [("dv", conv_dv[j])]
                    yield

                def ssd_prep(j, c):
                    slot = j % 2
                    p = c % 2
                    cc0 = c * 64
                    h0 = 2 * j
                    cdv = conv_dv[j]
                    m1 = env.op("tensor", lambda e, c=c, slot=slot: e.transpose(out=pbb[7][:, 0:128], in_=hconv[slot][:, c * CS:(c + 1) * CS], identity=eye_bf[:]),
                                waits=[("dv", cdv), ("dv", self_free["b7"])], incs=[("pe", 1)])
                    env.op("vector", lambda e, cc0=cc0, h0=h0, p=p: e.tensor_tensor(out=hct[p].ap().rearrange("p (h q) -> p h q", h=2),
                                                                                   in0=pbb[7][:, 0:128].rearrange("p (h q) -> p h q", h=2),
                                                                                   in1=dtp_tm[:, cc0 + h0:cc0 + h0 + 2].unsqueeze(2).broadcast_to((128, 2, 64)),
                                                                                   op=ALU.mult),
                           waits=[("pe", m1["pe"])], incs=[("dv", 1)])
                    env.op("vector", lambda e, h0=h0, p=p: e.tensor_tensor(out=dh[p].ap().rearrange("p (h q) -> p h q", h=2),
                                                                           in0=pbb[7][:, 0:128].rearrange("p (h q) -> p h q", h=2),
                                                                           in1=d_bc[:, h0:h0 + 2].unsqueeze(2).broadcast_to((128, 2, 64)),
                                                                           op=ALU.mult),
                           incs=[("dv", 1)])
                    self_free["b7"] = env.cnt["dv"]
                    env.op("vector", lambda e, cc0=cc0, h0=h0, p=p: e.tensor_tensor(out=segr[p].ap().rearrange("p (h q) -> p h q", h=2),
                                                                                    in0=tri_bf.ap().unsqueeze(1).broadcast_to((128, 2, 128)),
                                                                                    in1=ad_tm[:, cc0 + h0:cc0 + h0 + 2].unsqueeze(2).broadcast_to((128, 2, 128)),
                                                                                    op=ALU.mult),
                           incs=[("dv", 1)])
                    sgdv = env.cnt["dv"]
                    for hh in range(2):
                        env.op("tensor", lambda e, hh=hh, p=p: e.matmul(out=pb[5][:, hh * 128:(hh + 1) * 128], lhsT=ut_bf[:],
                                                                        rhs=segr[p][:, hh * 128:(hh + 1) * 128], start=True, stop=True),
                               waits=([("dv", sgdv), ("ac", self_free["seg_ac"])] if hh == 0 else []), incs=[("pe", 1)])
                        env.op("tensor", lambda e, hh=hh, p=p: e.matmul(out=pb[5][0:64, 256 + hh * 128:256 + (hh + 1) * 128], lhsT=ones_bf[:, 0:64],
                                                                        rhs=segr[p][:, hh * 128:(hh + 1) * 128], start=True, stop=True),
                               incs=[("pe", 1)])
                    segpe = env.cnt["pe"]
                    env.op("scalar", lambda e, p=p: e.activation(out=mt_sb[p][:], in_=pb[5][:, 0:256], func=AF.Exp), waits=[("pe", segpe)], incs=[("ac", 1)])
                    env.op("scalar", lambda e, p=p: e.activation(out=ms_sb[p][:], in_=pb[5][0:64, 256:512], func=AF.Exp), incs=[("ac", 1)])
                    self_free["seg_ac"] = env.cnt["ac"]
                    exp_ac = env.cnt["ac"]
                    env.op("vector", lambda e, c=c, p=p: e.tensor_tensor(out=mt_sb[p].ap().rearrange("p (h q) -> p h q", h=2),
                                                                         in0=mt_sb[p].ap().rearrange("p (h q) -> p h q", h=2),
                                                                         in1=gm_all[:, c * 128:(c + 1) * 128].unsqueeze(1).broadcast_to((128, 2, 128)),
                                                                         op=ALU.mult),
                           waits=[("ac", exp_ac)], incs=[("dv", 1)])
                    env.op("vector", lambda e, c=c, p=p: e.tensor_tensor(out=ms_sb[p].ap().rearrange("p (h q) -> p h q", h=2),
                                                                         in0=ms_sb[p].ap().rearrange("p (h q) -> p h q", h=2),
                                                                         in1=c_conv[:, c * CS:(c + 1) * CS].unsqueeze(1).broadcast_to((64, 2, 128)),
                                                                         op=ALU.mult),
                           incs=[("dv", 1)])
                    env.op("vector", lambda e, cc0=cc0, h0=h0, p=p: e.tensor_tensor(out=bdec[p].ap().rearrange("p (h q) -> p h q", h=2),
                                                                                    in0=btm_all[:, cc0:cc0 + 64].unsqueeze(1).broadcast_to((128, 2, 64)),
                                                                                    in1=dec_all[:, cc0 + h0:cc0 + h0 + 2].unsqueeze(2).broadcast_to((128, 2, 64)),
                                                                                    op=ALU.mult),
                           incs=[("dv", 1)])
                    return env.cnt["dv"]

                def ssd_finish(j, c, prep_dv, st_ctx):
                    slot = j % 2
                    p = c % 2
                    cc0 = c * 64
                    h0 = 2 * j
                    for hh in range(2):
                        env.op("tensor", lambda e, hh=hh, p=p: e.matmul(out=pb[6][:, hh * 64:(hh + 1) * 64], lhsT=mt_sb[p][:, hh * 128:(hh + 1) * 128],
                                                                        rhs=hct[p][:, hh * 64:(hh + 1) * 64], start=True, stop=False),
                               waits=([("dv", prep_dv), ("dv", st_ctx["st_dv"]), ("dv", self_free["y"])] if hh == 0 else []), incs=[("pe", 1)])
                        env.op("tensor", lambda e, hh=hh, p=p: e.matmul(out=pb[6][:, hh * 64:(hh + 1) * 64], lhsT=ms_sb[p][0:64, hh * 128:(hh + 1) * 128],
                                                                        rhs=st_cur[0:64, hh * 64:(hh + 1) * 64], start=False, stop=True),
                               incs=[("pe", 1)])
                        env.op("tensor", lambda e, hh=hh, p=p: e.matmul(out=pb[6][0:64, 128 + hh * 64:192 + hh * 64], lhsT=bdec[p][:, hh * 64:(hh + 1) * 64],
                                                                        rhs=hct[p][:, hh * 64:(hh + 1) * 64], start=True, stop=True),
                               incs=[("pe", 1)])
                    ype = env.cnt["pe"]
                    env.op("vector", lambda e, cc0=cc0, h0=h0: e.tensor_tensor(out=st_tmp.ap().rearrange("p (h q) -> p h q", h=2),
                                                                               in0=st_cur.ap().rearrange("p (h q) -> p h q", h=2),
                                                                               in1=ea_all[0:64, cc0 + h0:cc0 + h0 + 2].unsqueeze(2).broadcast_to((64, 2, 64)),
                                                                               op=ALU.mult),
                           waits=[("pe", ype)], incs=[("dv", 1)])
                    env.op("vector", lambda e: e.tensor_add(out=st_cur[:], in0=st_tmp[:], in1=pb[6][0:64, 128:256]), incs=[("dv", 1)])
                    st_ctx["st_dv"] = env.cnt["dv"]
                    env.op("vector", lambda e, p=p: e.tensor_add(out=ytm[:], in0=pb[6][:, 0:128], in1=dh[p][:]), incs=[("dv", 1)])
                    ytm_dv = env.cnt["dv"]
                    m9 = env.op("tensor", lambda e: e.transpose(out=pbb[6][:, 512:640], in_=ytm[:], identity=eye_bf[:]),
                                waits=[("dv", ytm_dv)], incs=[("pe", 1)])
                    env.op("vector", lambda e, c=c, slot=slot: e.tensor_copy(out=ystg[slot][:, (c % 8) * CS:(c % 8 + 1) * CS], in_=pbb[6][:, 512:640]),
                           waits=[("pe", m9["pe"])] + (ystg_free[slot] if c == 0 else []), incs=[("dv", 1)])
                    self_free["y"] = env.cnt["dv"]
                    if c == NCH // 2 - 1 or c == NCH - 1:
                        half = 0 if c == NCH // 2 - 1 else 1
                        env.op("sync", lambda e, j=j, slot=slot, half=half: e.dma_start(out=y_sp.ap()[j][:, half * 1024:(half + 1) * 1024],
                                                                                       in_=ystg[slot][:]),
                               waits=env.now("dv"), incs=[(f"qy{slot}", 16)])

                def gen_ssd(j):
                    slot = j % 2
                    msz = env.op("vector", lambda e: e.memset(st_cur[:], 0.0), incs=[("dv", 1)])
                    st_ctx = {"st_dv": msz["dv"]}
                    prep_dv = ssd_prep(j, 0)
                    yield
                    for c in range(NCH):
                        nprep_dv = ssd_prep(j, c + 1) if c + 1 < NCH else None
                        ssd_finish(j, c, prep_dv, st_ctx)
                        prep_dv = nprep_dv
                        yield
                    hconv_free[slot] = env.cnt["pe"]
                    ystg_free[slot] = [(f"qy{slot}", env.cnt[f"qy{slot}"])]
                    env.op("vector", lambda e, j=j: e.tensor_scalar(out=st_stage[:], in0=st_cur[:], scalar1=masks[0:64, 1:2], scalar2=None, op0=ALU.mult),
                           waits=[("dv", st_ctx["st_dv"])] + ststg_free[0], incs=[("dv", 1)])
                    mstg = env.op("sync", lambda e, j=j: e.dma_start(out=st_snd.ap()[:, j * 128:(j + 1) * 128], in_=st_stage[:]),
                                  waits=env.now("dv"), incs=[("qss", 16)])
                    ststg_free[0] = [("qss", mstg["qss"])]
                    if dbg_st is not None:
                        env.op("sync", lambda e, j=j: e.dma_start(out=dbg_st.ap()[:, j * 128:(j + 1) * 128], in_=st_cur[:]),
                               waits=[("dv", st_ctx["st_dv"])], incs=[("qdbg", 16)])
                    yield

                self_free = {"seg_ac": env.cnt["ac"], "y": env.cnt["dv"], "b7": env.cnt["dv"]}
                ststg_free = [[]]
                for _ in gen_p2a(0):
                    pass
                for j in range(NBLK):
                    ssd = gen_ssd(j)
                    nxt = gen_p2a(j + 1) if j + 1 < NBLK else iter(())
                    for c in range(NCH + 1):
                        if next(ssd, None) is StopIteration:
                            pass
                        for _ in range(3):
                            next(nxt, None)
                    for _ in nxt:
                        pass
                    for _ in ssd:
                        pass
                env.wait("gpsimd", env.now("qss"))
                env.op("gpsimd", lambda e: e.collective_compute("AllReduce", ALU.add,
                                                                replica_groups=[[0, 1], [2, 3], [4, 5], [6, 7]],
                                                                ins=[st_snd.ap().opt()], outs=[st_rcv.ap().opt()]),
                       incs=[("cc", 1)])
                env.wait("gpsimd", env.now("cc"))
                env.op("gpsimd", lambda e: e.collective_compute("AllReduce", ALU.add,
                                                                replica_groups=[[0, 1], [2, 3], [4, 5], [6, 7]],
                                                                ins=[flag_snd.ap().opt()], outs=[flag_rcv.ap().opt()]),
                       incs=[("cc", 1)])
                env.barrier()

        def emit_p3():
            with ExitStack() as ps:
                s3 = lambda name, shape, dtype: ps.enter_context(nc.sbuf_tensor(name + "_p3", shape, dtype))
                xg = s3("xg", [128, KD, G], F32)
                xb = s3("xb", [128, KD, G], BF16)
                s_bf = s3("s_bf", [64, NH * P], BF16)
                yhg = s3("yhg", [128, NBLK, 256], BF16)
                ghg = s3("ghg", [128, NBLK, 256], BF16)
                yfn = s3("yfn", [128, NBLK, G], BF16)
                dytm = s3("dytm", [128, 128], BF16)
                ytm3 = s3("ytm3", [128, 128], BF16)
                x2g = s3("x2g", [128, KD, G], F32)
                wot = [s3(f"wot{i}", [128, 8, 128], BF16) for i in range(2)]

                msr = env.op("gpsimd", lambda e: e.dma_start(out=s_bf[:], in_=st_rcv.ap()), waits=env.now("cc"), incs=[("qsb", 16)])
                env.op("vector", lambda e: e.tensor_scalar(out=s_bf[:], in0=s_bf[:], scalar1=masks[0:64, 0:1], scalar2=None, op0=ALU.mult),
                       waits=[("qsb", msr["qsb"])], incs=[("dv", 1)])
                sin_dv = env.cnt["dv"]
                dy_free = sin_dv
                tp_free = sin_dv
                op_free = [sin_dv] * 8
                wot_free = {0: [], 1: []}
                for g in range(NG):
                    mx = env.op("sync", lambda e, g=g: e.dma_start(out=xg.ap(),
                                                                   in_=x1_sp.ap()[:, :, g * G:(g + 1) * G].rearrange("k p g -> p k g")),
                                waits=env.now("dv", "ac"), incs=[("qx0", 16)])
                    for hg in range(2):
                        t0 = g * G + hg * 256
                        my = env.op("sync", lambda e, t0=t0: e.dma_start(out=yhg.ap(),
                                                                         in_=y_sp.ap()[:, :, t0:t0 + 256].rearrange("b p t -> p b t")),
                                    waits=env.now("dv", "ac"), incs=[(f"qp{hg}", 16)])
                        mg_ = env.op("sync", lambda e, t0=t0: e.dma_start(out=ghg.ap(),
                                                                          in_=gate_sp.ap()[:, :, t0:t0 + 256].rearrange("b p t -> p b t")),
                                     incs=[(f"qp{hg}", 16)])
                        ldw = [(f"qp{hg}", mg_[f"qp{hg}"])]
                        for bi in range(NBLK):
                            for ci in range(2):
                                c = (t0 + ci * CS) // CS
                                cc0 = c * 64
                                for hh in range(2):
                                    env.op("tensor", lambda e, bi=bi, c=c, hh=hh: e.matmul(out=pb[4][:, hh * 64:(hh + 1) * 64],
                                                                                           lhsT=c_conv[:, c * CS:(c + 1) * CS],
                                                                                           rhs=s_bf[0:64, (2 * bi + hh) * 64:(2 * bi + hh + 1) * 64],
                                                                                           start=True, stop=True),
                                           waits=([("dv", dy_free)] + ldw if hh == 0 else []), incs=[("pe", 1)])
                                dype = env.cnt["pe"]
                                for hh in range(2):
                                    env.op("vector", lambda e, bi=bi, cc0=cc0, hh=hh: e.tensor_scalar(out=dytm[:, hh * 64:(hh + 1) * 64],
                                                                                                      in0=pb[4][:, hh * 64:(hh + 1) * 64],
                                                                                                      scalar1=lam_all[:, cc0 + 2 * bi + hh:cc0 + 2 * bi + hh + 1],
                                                                                                      scalar2=None, op0=ALU.mult),
                                           waits=([("pe", dype)] if hh == 0 else []), incs=[("dv", 1)])
                                dy_free = env.cnt["dv"]
                                m9 = env.op("tensor", lambda e: e.transpose(out=pbb[5][:, 0:128], in_=dytm[:], identity=eye_bf[:]),
                                            waits=[("dv", dy_free), ("dv", tp_free)], incs=[("pe", 1)])
                                env.op("vector", lambda e, bi=bi, ci=ci: e.tensor_add(out=ytm3[:], in0=pbb[5][:, 0:128], in1=yhg[:, bi, ci * CS:(ci + 1) * CS]),
                                       waits=[("pe", m9["pe"])], incs=[("dv", 1)])
                                tp_free = env.cnt["dv"]
                                env.op("vector", lambda e, bi=bi, ci=ci: e.tensor_mul(out=ghg[:, bi, ci * CS:(ci + 1) * CS], in0=ytm3[:], in1=ghg[:, bi, ci * CS:(ci + 1) * CS]),
                                       incs=[("dv", 1)])
                        yf_dv = env.cnt["dv"]
                        env.op("scalar", lambda e: e.activation(out=yhg.ap().rearrange("p b t -> p (b t)"), in_=ghg.ap().rearrange("p b t -> p (b t)"),
                                                                func=AF.Square), waits=[("dv", yf_dv)], incs=[("ac", 1)])
                        sq_ac = env.cnt["ac"]
                        for bi in range(NBLK):
                            env.op("tensor", lambda e, bi=bi: e.matmul(out=pb[6][0:1, 0:256], lhsT=ones_bf[:, 0:1], rhs=yhg[:, bi, :],
                                                                       start=(bi == 0), stop=(bi == NBLK - 1)),
                                   waits=([("ac", sq_ac)] if bi == 0 else []), incs=[("pe", 1)])
                        env.op("scalar", lambda e: e.copy(out=rms_row[:, 0:256], in_=pb[6][0:1, 0:256]), waits=env.now("pe"), incs=[("ac", 1)])
                        env.op("vector", lambda e: e.tensor_scalar(out=rms_row[:, 0:256], in0=rms_row[:, 0:256], scalar1=1.0 / E, scalar2=None, op0=ALU.mult),
                               waits=env.now("ac"), incs=[("dv", 1)])
                        env.op("scalar", lambda e: e.activation(out=rms_row[:, 0:256], in_=rms_row[:, 0:256], func=AF.Ln, bias=RMS_EPS),
                               waits=env.now("dv"), incs=[("ac", 1)])
                        env.op("scalar", lambda e: e.activation(out=rms_row[:, 256:512], in_=rms_row[:, 0:256], func=AF.Exp, scale=-0.5), incs=[("ac", 1)])
                        mrb = env.op("tensor", lambda e: e.matmul(out=pb[7][:, 0:256], lhsT=ones_f[0:1, :], rhs=rms_row[:, 256:512], start=True, stop=True),
                                     waits=env.now("ac", "dv"), incs=[("pe", 1)])
                        for bi in range(NBLK):
                            env.op("vector", lambda e, bi=bi, hg=hg: e.scalar_tensor_tensor(out=yfn[:, bi, hg * 256:(hg + 1) * 256], in0=ghg[:, bi, :],
                                                                                            scalar=nw[:, bi:bi + 1], in1=pb[7][:, 0:256],
                                                                                            op0=ALU.mult, op1=ALU.mult),
                                   waits=([("pe", mrb["pe"])] if bi == 0 else []), incs=[("dv", 1)])
                    yfn_dv = env.cnt["dv"]
                    for mh in range(2):
                        for f in range(32):
                            slot = f % 2
                            mw = env.op("gpsimd", lambda e, f=f, mh=mh, slot=slot: e.dma_start(out=wot[slot].ap().rearrange("p a b -> p (a b)"),
                                                                                              in_=woT.ap()[f][:, mh * 8:(mh + 1) * 8, :].rearrange("p a b -> p (a b)")),
                                         waits=wot_free[slot], incs=[(f"qo{slot}", 16)])
                            for m8 in range(8):
                                env.op("tensor", lambda e, f=f, m8=m8, slot=slot: e.matmul(out=pb[m8][:, 0:G], lhsT=wot[slot][:, m8, :], rhs=yfn[:, f, :],
                                                                                           start=(f == 0), stop=(f == 31)),
                                       waits=(([(f"qo{slot}", mw[f"qo{slot}"]), ("dv", yfn_dv), ("dv", max(op_free))]) if m8 == 0 and f == 0 else
                                              ([(f"qo{slot}", mw[f"qo{slot}"])] if m8 == 0 else [])), incs=[("pe", 1)])
                            wot_free[slot] = [("pe", env.cnt["pe"])]
                        oppe = env.cnt["pe"]
                        for m8 in range(8):
                            kd = mh * 8 + m8
                            env.op("vector", lambda e, m8=m8, kd=kd: e.tensor_add(out=x2g[:, kd, :], in0=pb[m8][:, 0:G], in1=xg[:, kd, :]),
                                   waits=[("pe", oppe)], incs=[("dv", 1)])
                            op_free[m8] = env.cnt["dv"]
                    dy_free = env.cnt["dv"]
                    tp_free = env.cnt["dv"]
                    x2_dv = env.cnt["dv"]
                    env.op("sync", lambda e, g=g: e.dma_start(out=x2_sp.ap()[:, :, g * G:(g + 1) * G].rearrange("k p g -> p k g"),
                                                              in_=x2g.ap()),
                           waits=[("dv", x2_dv)], incs=[("qst", 16)])
                    ln_stats(x2g, g, mu3_row, rstd3_row, LN_EPS, xb, xb_extra=[("dv", x2_dv)])
                env.barrier()

        emit_ffn(x_in, x1_sp, None, (mu2_row, rstd2_row, LN_EPS), first_wait=prep_done, do_halo=(upto >= 2))
        if upto >= 2:
            emit_p2()
        if upto >= 3:
            emit_p3()
        p23.close()
        if upto >= 4:
            emit_ffn(x2_sp, out_d, (mu3_row, rstd3_row), None, first_wait=env.now("dv", "ac", "qs", "pe"))
        env.wait("sync", [(n, env.cnt[n]) for n in env.sems if n.startswith("q")])
        env.wait("gpsimd", [(n, env.cnt[n]) for n in env.sems if n.startswith("q")])
        env.replay(ctx.enter_context(nc.Block()))
    return nc


def _prep_inputs(inputs):
    x = np.asarray(inputs["L0_attn_x"])
    wi = np.asarray(inputs["L0_attn_in_proj_w"])
    wo = np.asarray(inputs["L0_attn_out_proj_w"])
    cw = np.asarray(inputs["L0_attn_conv_w"])
    cb = np.asarray(inputs["L0_attn_conv_b"])
    dtbv = np.asarray(inputs["L0_attn_dt_bias"])
    alog = np.asarray(inputs["L0_attn_A_log"])
    dd = np.asarray(inputs["L0_attn_D"])
    nwv = np.asarray(inputs["L0_attn_norm_w"])
    wqg = np.asarray(inputs["L0_ffn_wq_gate"])
    sgv = np.asarray(inputs["L0_ffn_s_gate"])
    wqu = np.asarray(inputs["L0_ffn_wq_up"])
    suv = np.asarray(inputs["L0_ffn_s_up"])
    wqd = np.asarray(inputs["L0_ffn_wq_down"])
    sdv = np.asarray(inputs["L0_ffn_s_down"])

    def u8(a):
        return np.ascontiguousarray(np.ascontiguousarray(a).view("<u1").reshape(*a.shape, 4)[..., 0])

    wiT = np.ascontiguousarray(np.ascontiguousarray(wi.T).reshape(KD, 128, PROJ).transpose(1, 0, 2))
    woT = np.ascontiguousarray(np.ascontiguousarray(wo.T).reshape(32, 128, KD, 128))
    wg8a = np.ascontiguousarray(u8(wqg).T.reshape(KD, 128, KH, 128).transpose(2, 1, 0, 3))
    wu8a = np.ascontiguousarray(u8(wqu).T.reshape(KD, 128, KH, 128).transpose(2, 1, 0, 3))
    wd8a = np.ascontiguousarray(u8(wqd).T.reshape(KH, 128, KD, 128))

    consts = np.zeros((6, 128, 128), np.float32)
    consts[0] = np.eye(128, dtype=np.float32)
    idx = np.arange(128)
    consts[1] = (idx[:, None] <= idx[None, :]).astype(np.float32)
    consts[2] = (idx[:, None] > idx[None, :]).astype(np.float32)
    consts[3] = 1.0
    consts[4] = -128.0
    consts[5] = 0.5

    common = {
        "wiT": wiT, "woT": woT, "wg8": wg8a, "wu8": wu8a, "wd8": wd8a, "consts": consts,
        "convw": np.ascontiguousarray(cw.reshape(33, 128, KCONV).transpose(1, 0, 2).reshape(128, 33 * KCONV)),
        "convb": np.ascontiguousarray(cb.reshape(33, 128).T),
        "dtb": np.ascontiguousarray(dtbv.reshape(64, 1)),
        "alog": np.ascontiguousarray(alog.reshape(64, 1)),
        "drow": np.ascontiguousarray(dd.reshape(1, 64)),
        "nw": np.ascontiguousarray(nwv.reshape(32, 128).T),
        "sg": np.ascontiguousarray(sgv.reshape(KH, 128).T),
        "su": np.ascontiguousarray(suv.reshape(KH, 128).T),
        "sd": np.ascontiguousarray(sdv.reshape(KD, 128).T),
    }
    in_maps = []
    for c in range(8):
        b, h = c // 2, c % 2
        xs = np.ascontiguousarray(np.ascontiguousarray(x[b, h * TOK:(h + 1) * TOK, :].T).reshape(KD, 128, TOK))
        m = dict(common)
        m["x_in"] = xs
        m["fconst"] = np.tile(np.array([[1e-5, 1e-6, 1.0, -0.5, -1.0, 0.5, 0.0, 2.0]], np.float32), (128, 1))
        msk = np.zeros((128, 2), np.float32)
        msk[:, 0] = 1.0 if h == 1 else 0.0
        msk[:, 1] = 1.0 if h == 0 else 0.0
        m["masks"] = msk
        in_maps.append(m)
    return in_maps


def kernel(**inputs):
    in_maps = _prep_inputs(inputs)
    nc = build_program()
    res = run_bass_kernel_spmd(nc, in_maps, core_ids=list(range(8)))
    out = np.empty((B, S, D), np.float32)
    for c in range(8):
        b, h = c // 2, c % 2
        o = res.results[c]["out"]
        out[b, h * TOK:(h + 1) * TOK, :] = o.reshape(D, TOK).T
    return out


# version 08 (final)
# speedup vs baseline: 1.1852x; 1.1852x over previous
import numpy as np
import ml_dtypes
from contextlib import ExitStack

import concourse.bass as bass
import concourse.mybir as mybir
from concourse.bass_utils import run_bass_kernel_spmd

F32 = mybir.dt.float32
BF16 = mybir.dt.bfloat16
U8 = mybir.dt.uint8
AF = mybir.ActivationFunctionType
ALU = mybir.AluOpType

B, S, D = 4, 4096, 2048
E = 2 * D
NH, P, N = 64, 64, 64
KCONV = 4
CONV_DIM = E + 2 * N
PROJ = E + CONV_DIM + NH
HID = 5632
TOK = 2048
CS = 128
NCH = TOK // CS
G = 512
NG = TOK // G
KD = D // 128
KH = HID // 128
NBLK = 32
LN_EPS = 1e-5
RMS_EPS = 1e-6
FCONSTS = (1e-5, 1e-6, 1.0, -0.5, -1.0, 0.5, 0.0, 2.0)

ENGINES = ("sync", "tensor", "vector", "scalar", "gpsimd")
SEMS = ("qs", "qw", "dv", "ac", "pe", "cc")


class Env:
    def __init__(self, nc, ctx):
        self.nc = nc
        self.ctx = ctx
        self.sems = {}
        self.cnt = {}
        self.streams = {e: [] for e in ENGINES}

    def _sem(self, name):
        if name not in self.sems:
            self.sems[name] = self.ctx.enter_context(self.nc.semaphore(name))
            self.cnt[name] = 0
        return self.sems[name]

    def op(self, engine, fn, waits=(), incs=()):
        for s, v in waits:
            self._sem(s)
            assert v <= self.cnt[s], f"wait on future milestone {s}:{v} (now {self.cnt[s]})"
        ms = {}
        for sname, k in incs:
            self._sem(sname)
            self.cnt[sname] += k
            ms[sname] = self.cnt[sname]
        self.streams[engine].append((tuple(waits), fn, tuple(incs)))
        return ms

    def wait(self, engine, waits):
        self.op(engine, None, waits=waits)

    def now(self, *names):
        return [(n, self.cnt.get(n, 0)) for n in names]

    def barrier(self):
        b = [(n, self.cnt[n]) for n in self.sems]
        for e in ENGINES:
            self.wait(e, b)

    def replay(self, block):
        for ename in ENGINES:
            stream = self.streams[ename]
            if not stream:
                continue

            def emit(eng, stream=stream):
                seen = {}
                for waits, fn, incs in stream:
                    for sname, val in waits:
                        if seen.get(sname, -1) < val:
                            eng.wait_ge(self.sems[sname], val)
                            seen[sname] = val
                    if fn is None:
                        continue
                    inst = fn(eng)
                    for sname, k in incs:
                        inst.then_inc(self.sems[sname], k)

            getattr(block, ename)(emit)


def build_program(dbg=(), upto=4):
    nc = bass.Bass("TRN2", target_bir_lowering=False)
    dram = lambda name, shape, dtype, kind="Internal": nc.dram_tensor(name, shape, dtype, kind=kind)

    x_in = dram("x_in", [KD, 128, TOK], F32, kind="ExternalInput")
    wiT = dram("wiT", [128, KD, PROJ], F32, kind="ExternalInput")
    woT = dram("woT", [32, 128, KD, 128], F32, kind="ExternalInput")
    wg8 = dram("wg8", [KH, 128, KD, 128], U8, kind="ExternalInput")
    wu8 = dram("wu8", [KH, 128, KD, 128], U8, kind="ExternalInput")
    wd8 = dram("wd8", [KH, 128, KD, 128], U8, kind="ExternalInput")
    consts = dram("consts", [6, 128, 128], F32, kind="ExternalInput")
    convw_in = dram("convw", [128, 33 * KCONV], F32, kind="ExternalInput")
    convb_in = dram("convb", [128, 33], F32, kind="ExternalInput")
    dtb_in = dram("dtb", [64, 1], F32, kind="ExternalInput")
    alog_in = dram("alog", [64, 1], F32, kind="ExternalInput")
    drow_in = dram("drow", [1, 64], F32, kind="ExternalInput")
    nw_in = dram("nw", [128, 32], F32, kind="ExternalInput")
    sg_in = dram("sg", [128, KH], F32, kind="ExternalInput")
    su_in = dram("su", [128, KH], F32, kind="ExternalInput")
    sd_in = dram("sd", [128, KD], F32, kind="ExternalInput")
    masks_in = dram("masks", [128, 2], F32, kind="ExternalInput")
    fconst_in = dram("fconst", [128, 8], F32, kind="ExternalInput")

    out_d = dram("out", [KD, 128, TOK], F32, kind="ExternalOutput")

    def spill(name, shape, dtype):
        return dram(name, shape, dtype, kind=("ExternalOutput" if name in dbg else "Internal"))

    x1_sp = spill("x1_sp", [KD, 128, TOK], F32)
    x2_sp = spill("x2_sp", [KD, 128, TOK], F32)
    y_sp = spill("y_sp", [NBLK, 128, TOK], BF16)
    gate_sp = spill("gate_sp", [NBLK, 128, TOK], BF16)
    dbg_bc = spill("dbg_bc", [128, TOK], BF16) if "dbg_bc" in dbg else None
    dbg_dt = spill("dbg_dt", [64, TOK], BF16) if "dbg_dt" in dbg else None
    dbg_st = spill("dbg_st", [64, NH * P], F32) if "dbg_st" in dbg else None
    dbg_halo = spill("dbg_halo", [128, KD * 3], F32) if "dbg_halo" in dbg else None
    dbg_xc = spill("dbg_xc", [128, KD, TOK], BF16) if "dbg_xc" in dbg else None

    halo_snd = dram("halo_snd", [128, KD * 3], F32)
    halo_rcv = dram("halo_rcv", [128, KD * 3], F32)
    st_snd = dram("st_snd", [64, NH * P], F32)
    flag_snd = dram("flag_snd", [1, 8], F32)
    flag_rcv = dram("flag_rcv", [1, 8], F32)
    st_rcv = dram("st_rcv", [64, NH * P], F32)

    with ExitStack() as ctx:
        env = Env(nc, ctx)
        sb = lambda name, shape, dtype: ctx.enter_context(nc.sbuf_tensor(name, shape, dtype))
        pb = [ctx.enter_context(nc.psum_tensor(f"pb{i}", [128, 512], F32)) for i in range(8)]
        pbb = [t.ap().bitcast(BF16) for t in pb]

        eye_bf = sb("eye_bf", [128, 128], BF16)
        tri_f = sb("tri_f", [128, 128], F32)
        tri_bf = sb("tri_bf", [128, 128], BF16)
        ut_bf = sb("ut_bf", [128, 128], BF16)
        ones_bf = sb("ones_bf", [128, 128], BF16)
        neg128_bf = sb("neg128_bf", [1, 128], BF16)
        ones_f = sb("ones_f", [65, 128], F32)

        def ones_row(row_ap):
            bp = row_ap.base_partition()
            return ones_f[bp:bp + 1, :]
        convw = sb("convw_sb", [128, 33 * KCONV], F32)
        convb = sb("convb_sb", [128, 33], F32)
        dtb = sb("dtb_sb", [64, 1], F32)
        a_bc = sb("a_bc", [128, 64], BF16)
        d_bc = sb("d_bc", [128, 64], BF16)
        nw = sb("nw_sb", [128, 32], F32)
        sg = sb("sg_sb", [128, KH], F32)
        su = sb("su_sb", [128, KH], F32)
        sdh = sb("sdh_sb", [128, KD], F32)
        masks = sb("masks_sb", [128, 2], F32)
        d_row_sb = sb("d_row_sb", [1, 64], F32)
        a_row_f = sb("a_row_f", [1, 64], F32)
        fconst = sb("fconst_sb", [128, 8], F32)
        ach_row = sb("ach_row", [1, 64], F32)
        ea_row = sb("ea_row", [1, 64], F32)
        runa_row = sb("runa_row", [1, 64], F32)
        hrow = sb("hrow", [1, 16], F32)
        rms_row = sb("rms_row", [1, 512], F32)
        sum_bf = sb("sum_bf", [1, 2 * G], BF16)
        rowsA = sb("rowsA", [65, TOK], F32)
        rowsB = sb("rowsB", [65, TOK], F32)
        rowsS = sb("rowsS", [65, 4 * G], F32)
        mu_t_row = rowsA.ap()[0:1, :]
        rstd_t_row = rowsA.ap()[32:33, :]
        mu2_row = rowsA.ap()[64:65, :]
        rstd2_row = rowsB.ap()[0:1, :]
        mu3_row = rowsB.ap()[32:33, :]
        rstd3_row = rowsB.ap()[64:65, :]
        stat_row = rowsS.ap()[0:1, :]

        cst = consts.ap()
        env.op("gpsimd", lambda e: e.dma_start(out=eye_bf[:], in_=cst[0]), incs=[("qw", 16)])
        env.op("sync", lambda e: e.dma_start(out=tri_f[:], in_=cst[1]), incs=[("qs", 16)])
        env.op("gpsimd", lambda e: e.dma_start(out=tri_bf[:], in_=cst[1]), incs=[("qw", 16)])
        env.op("gpsimd", lambda e: e.dma_start(out=ut_bf[:], in_=cst[2]), incs=[("qw", 16)])
        env.op("gpsimd", lambda e: e.dma_start(out=ones_bf[:], in_=cst[3]), incs=[("qw", 16)])
        env.op("gpsimd", lambda e: e.dma_start(out=neg128_bf[:], in_=cst[4][0:1, :]), incs=[("qw", 16)])
        env.op("sync", lambda e: e.dma_start(out=ones_f[:], in_=cst[3][0:65, :]), incs=[("qs", 16)])
        env.op("sync", lambda e: e.dma_start(out=convw[:], in_=convw_in.ap()), incs=[("qs", 16)])
        env.op("sync", lambda e: e.dma_start(out=convb[:], in_=convb_in.ap()), incs=[("qs", 16)])
        env.op("sync", lambda e: e.dma_start(out=dtb[:], in_=dtb_in.ap()), incs=[("qs", 16)])
        env.op("sync", lambda e: e.dma_start(out=nw[:], in_=nw_in.ap()), incs=[("qs", 16)])
        env.op("sync", lambda e: e.dma_start(out=sg[:], in_=sg_in.ap()), incs=[("qs", 16)])
        env.op("sync", lambda e: e.dma_start(out=su[:], in_=su_in.ap()), incs=[("qs", 16)])
        env.op("sync", lambda e: e.dma_start(out=sdh[:], in_=sd_in.ap()), incs=[("qs", 16)])
        env.op("sync", lambda e: e.dma_start(out=masks[:], in_=masks_in.ap()), incs=[("qs", 16)])
        env.op("sync", lambda e: e.dma_start(out=d_row_sb[:], in_=drow_in.ap()), incs=[("qs", 16)])
        env.op("sync", lambda e: e.dma_start(out=a_row_f[:], in_=alog_in.ap().rearrange("p one -> one p")), incs=[("qs", 16)])
        env.op("sync", lambda e: e.dma_start(out=fconst[:], in_=fconst_in.ap()), incs=[("qs", 16)])
        for i, v in enumerate(FCONSTS):
            nc.const_aps.register(F32, float(v), fconst[:, i:i + 1])
        w0 = env.now("qs", "qw")
        env.op("vector", lambda e: e.tensor_scalar(out=convw[:], in0=convw[:], scalar1=0.5, scalar2=None, op0=ALU.mult), waits=w0, incs=[("dv", 1)])
        env.op("vector", lambda e: e.tensor_scalar(out=convb[:], in0=convb[:], scalar1=0.5, scalar2=None, op0=ALU.mult), incs=[("dv", 1)])
        env.op("vector", lambda e: e.tensor_scalar(out=sdh[:], in0=sdh[:], scalar1=0.5, scalar2=None, op0=ALU.mult), incs=[("dv", 1)])
        env.op("scalar", lambda e: e.activation(out=a_row_f[:], in_=a_row_f[:], func=AF.Exp), waits=w0, incs=[("ac", 1)])
        env.op("vector", lambda e: e.tensor_scalar(out=a_row_f[:], in0=a_row_f[:], scalar1=-1.0, scalar2=None, op0=ALU.mult), waits=env.now("ac"), incs=[("dv", 1)])
        env.op("vector", lambda e: e.memset(runa_row[:], 0.0), incs=[("dv", 1)])
        env.op("tensor", lambda e: e.matmul(out=pb[6][:, 0:64], lhsT=ones_f[0:1, :], rhs=a_row_f[:], start=True, stop=True), waits=env.now("dv", "qs"), incs=[("pe", 1)])
        env.op("tensor", lambda e: e.matmul(out=pb[6][:, 64:128], lhsT=ones_f[0:1, :], rhs=d_row_sb[:], start=True, stop=True), incs=[("pe", 1)])
        env.op("vector", lambda e: e.tensor_copy(out=a_bc[:], in_=pb[6][:, 0:64]), waits=env.now("pe"), incs=[("dv", 1)])
        env.op("vector", lambda e: e.tensor_copy(out=d_bc[:], in_=pb[6][:, 64:128]), incs=[("dv", 1)])
        prep_done = env.now("dv", "ac", "pe", "qs", "qw")

        def ln_stats(src, g, mu_row, rstd_row, eps, xb, xb_extra=()):
            srcf = src.ap().rearrange("p k g -> p (k g)")
            xbf = xb.ap().rearrange("p k g -> p (k g)")
            mdv = env.op("vector", lambda e: e.tensor_copy(out=xbf, in_=srcf), incs=[("dv", 1)], waits=list(xb_extra))
            for k in range(KD):
                env.op("tensor", lambda e, k=k: e.matmul(out=pb[6][0:1, 0:G], lhsT=ones_bf[:, 0:1], rhs=xb[:, k, :], start=(k == 0), stop=(k == KD - 1)),
                       waits=([("dv", mdv["dv"])] + env.now("ac") if k == 0 else []), incs=[("pe", 1)])
            sxpe = env.cnt["pe"]
            e1 = env.op("scalar", lambda e: e.copy(out=stat_row[:, 0:G], in_=pb[6][0:1, 0:G]), waits=[("pe", sxpe)], incs=[("ac", 1)])
            mac = env.op("scalar", lambda e: e.activation(out=xbf, in_=srcf, func=AF.Square), waits=[("pe", sxpe)] + list(xb_extra), incs=[("ac", 1)])
            for k in range(KD):
                env.op("tensor", lambda e, k=k: e.matmul(out=pb[6][0:1, 0:G], lhsT=ones_bf[:, 0:1], rhs=xb[:, k, :], start=(k == 0), stop=(k == KD - 1)),
                       waits=([("ac", mac["ac"])] if k == 0 else []), incs=[("pe", 1)])
            e2 = env.op("scalar", lambda e: e.copy(out=stat_row[:, G:2 * G], in_=pb[6][0:1, 0:G]), waits=env.now("pe"), incs=[("ac", 1)])
            env.op("vector", lambda e, g=g: e.tensor_scalar(out=mu_row[:, g * G:(g + 1) * G], in0=stat_row[:, 0:G], scalar1=1.0 / D, scalar2=None, op0=ALU.mult),
                   waits=[("ac", e1["ac"])], incs=[("dv", 1)])
            mmu = env.cnt["dv"]
            env.op("scalar", lambda e, g=g: e.activation(out=stat_row[:, 2 * G:3 * G], in_=mu_row[:, g * G:(g + 1) * G], func=AF.Square),
                   waits=[("dv", mmu)], incs=[("ac", 1)])
            env.op("vector", lambda e: e.tensor_scalar(out=stat_row[:, G:2 * G], in0=stat_row[:, G:2 * G], scalar1=1.0 / D, scalar2=None, op0=ALU.mult),
                   waits=[("ac", e2["ac"])], incs=[("dv", 1)])
            env.op("vector", lambda e: e.tensor_sub(out=stat_row[:, 3 * G:4 * G], in0=stat_row[:, G:2 * G], in1=stat_row[:, 2 * G:3 * G]),
                   waits=env.now("ac"), incs=[("dv", 1)])
            env.op("scalar", lambda e: e.activation(out=stat_row[:, 3 * G:4 * G], in_=stat_row[:, 3 * G:4 * G], func=AF.Ln, bias=float(eps)),
                   waits=env.now("dv"), incs=[("ac", 1)])
            env.op("scalar", lambda e, g=g: e.activation(out=rstd_row[:, g * G:(g + 1) * G], in_=stat_row[:, 3 * G:4 * G], func=AF.Exp, scale=-0.5),
                   incs=[("ac", 1)])
            return {"mu_dv": mmu, "rstd_ac": env.cnt["ac"], "xb_done": env.now("pe", "ac")}

        _uniq = [0]

        def emit_ffn(src_dram, dst_dram, stats, next_ln, first_wait=(), do_halo=False):
            _uniq[0] += 1
            _u = _uniq[0]
            with ExitStack() as fs:
                fsb = lambda name, shape, dtype: fs.enter_context(nc.sbuf_tensor(f"{name}_u{_u}", shape, dtype))
                xg = fsb("xg", [128, KD, G], F32)
                xb = fsb("xb", [128, KD, G], BF16)
                x1g = fsb("x1g", [128, KD, G], F32)
                hid = fsb("hid", [128, KH, G], BF16)
                g_pre = fsb("g_pre", [128, G], BF16)
                g_act = fsb("g_act", [128, G], BF16)
                u_pre = fsb("u_pre", [128, G], BF16)
                rstd_bc_sb = fsb("rstd_bc_sb", [128, G], F32)
                halo_stg = fsb("halo_stg", [128, KD * 3], F32) if do_halo else None
                wgt = [fsb(f"wgt{i}", [128, KD, 128], BF16) for i in range(3)]
                wut = [fsb(f"wut{i}", [128, KD, 128], BF16) for i in range(3)]
                wdt = [fsb(f"wdt{i}", [128, 4, 128], BF16) for i in range(6)]

                slot_free = {0: [], 1: [], 2: []}
                dn_slot_free = {i: [] for i in range(6)}
                x1g_free = []
                hid_free_pe = 0
                for g in range(NG):
                    wx = list(first_wait) if g == 0 else env.now("dv", "ac")
                    mx = env.op("sync", lambda e, g=g: e.dma_start(out=xg.ap(),
                                                                   in_=src_dram.ap()[:, :, g * G:(g + 1) * G].rearrange("k p g -> p k g")),
                                waits=wx, incs=[("qx0", 16)])
                    xw = [("qx0", mx["qx0"])]
                    if stats is None:
                        st = ln_stats(xg, g, mu_t_row, rstd_t_row, LN_EPS, xb, xb_extra=xw)
                        mu_row, rstd_row = mu_t_row, rstd_t_row
                        rstd_w = [("ac", st["rstd_ac"])]
                    else:
                        mu_row, rstd_row = stats
                        st = {"mu_dv": env.cnt["dv"]}
                        rstd_w = []
                    mb = env.op("tensor", lambda e, g=g, mu_row=mu_row: e.matmul(out=pb[7][:, 0:G], lhsT=ones_row(mu_row), rhs=mu_row[:, g * G:(g + 1) * G], start=True, stop=True),
                                waits=[("dv", st["mu_dv"])] + env.now("dv") + (list(first_wait) if g == 0 and stats is not None else []), incs=[("pe", 1)])
                    env.op("vector", lambda e: e.tensor_tensor(out=xb.ap(), in0=xg.ap(),
                                                               in1=pb[7][:, 0:G].unsqueeze(1).broadcast_to((128, KD, G)),
                                                               op=ALU.subtract),
                           waits=[("pe", mb["pe"])] + xw + env.now("ac"), incs=[("dv", 1)])
                    xc_dv = env.cnt["dv"]
                    mb2 = env.op("tensor", lambda e, g=g, rstd_row=rstd_row: e.matmul(out=pb[7][:, 0:G], lhsT=ones_row(rstd_row), rhs=rstd_row[:, g * G:(g + 1) * G], start=True, stop=True),
                                 waits=[("dv", xc_dv)] + rstd_w, incs=[("pe", 1)])
                    env.op("vector", lambda e: e.tensor_copy(out=rstd_bc_sb[:], in_=pb[7][:, 0:G]), waits=[("pe", mb2["pe"])], incs=[("dv", 1)])
                    rbc_dv = env.cnt["dv"]
                    for k in range(KD):
                        env.op("tensor", lambda e, k=k: e.matmul(out=pb[6][0:1, 0:G], lhsT=ones_bf[:, 0:1], rhs=xb[:, k, :], start=(k == 0), stop=(k == KD - 1)),
                               waits=([("dv", xc_dv)] + env.now("ac") if k == 0 else []), incs=[("pe", 1)])
                    env.op("scalar", lambda e: e.copy(out=sum_bf[:, 0:G], in_=pb[6][0:1, 0:G]), waits=env.now("pe"), incs=[("ac", 1)])
                    sx_ac = env.cnt["ac"]

                    prev_gate_dv = {0: 0, 1: 0}
                    prev_up_dv = {0: 0, 1: rbc_dv}
                    for kh in range(KH):
                        slot = kh % 3
                        par = kh % 2
                        bg = 0 if par == 0 else 6
                        bu = 1 if par == 0 else 7
                        mwg = env.op("gpsimd", lambda e, kh=kh, slot=slot: e.dma_start(out=wgt[slot].ap().rearrange("p k h -> p (k h)"),
                                                                                      in_=wg8.ap()[kh].rearrange("p k h -> p (k h)")),
                                     waits=slot_free[slot], incs=[(f"qa{slot}", 16)])
                        mwu = env.op("gpsimd", lambda e, kh=kh, slot=slot: e.dma_start(out=wut[slot].ap().rearrange("p k h -> p (k h)"),
                                                                                      in_=wu8.ap()[kh].rearrange("p k h -> p (k h)")),
                                     incs=[(f"qb{slot}", 16)])
                        for k in range(KD):
                            env.op("tensor", lambda e, k=k, slot=slot, bg=bg: e.matmul(out=pb[bg][:, 0:G], lhsT=wgt[slot][:, k, :], rhs=xb[:, k, :],
                                                                                start=(k == 0), stop=False),
                                   waits=([(f"qa{slot}", mwg[f"qa{slot}"]), ("ac", sx_ac), ("dv", prev_gate_dv[par])] if k == 0 else []), incs=[("pe", 1)])
                        env.op("tensor", lambda e, bg=bg: e.matmul(out=pb[bg][:, 0:G], lhsT=neg128_bf[:], rhs=sum_bf[:, 0:G], start=False, stop=True), incs=[("pe", 1)])
                        gmm = env.cnt["pe"]
                        for k in range(KD):
                            env.op("tensor", lambda e, k=k, slot=slot, bu=bu: e.matmul(out=pb[bu][:, 0:G], lhsT=wut[slot][:, k, :], rhs=xb[:, k, :],
                                                                                start=(k == 0), stop=False),
                                   waits=([(f"qb{slot}", mwu[f"qb{slot}"]), ("dv", prev_up_dv[par])] if k == 0 else []), incs=[("pe", 1)])
                        env.op("tensor", lambda e, bu=bu: e.matmul(out=pb[bu][:, 0:G], lhsT=neg128_bf[:], rhs=sum_bf[:, 0:G], start=False, stop=True), incs=[("pe", 1)])
                        umm = env.cnt["pe"]
                        slot_free[slot] = [("pe", umm)]
                        env.op("vector", lambda e, kh=kh, bg=bg: e.scalar_tensor_tensor(out=g_pre[:], in0=pb[bg][:, 0:G], scalar=sg[:, kh:kh + 1],
                                                                                 in1=rstd_bc_sb[:], op0=ALU.mult, op1=ALU.mult),
                               waits=[("pe", gmm), ("dv", rbc_dv)], incs=[("dv", 1)])
                        prev_gate_dv[par] = env.cnt["dv"]
                        env.op("scalar", lambda e: e.activation(out=g_act[:], in_=g_pre[:], func=AF.Silu), waits=env.now("dv"), incs=[("ac", 1)])
                        env.op("vector", lambda e, kh=kh, bu=bu: e.scalar_tensor_tensor(out=u_pre[:], in0=pb[bu][:, 0:G], scalar=su[:, kh:kh + 1],
                                                                                 in1=rstd_bc_sb[:], op0=ALU.mult, op1=ALU.mult),
                               waits=[("pe", umm)], incs=[("dv", 1)])
                        prev_up_dv[par] = env.cnt["dv"]
                        env.op("vector", lambda e, kh=kh: e.tensor_mul(out=hid[:, kh, :], in0=g_act[:], in1=u_pre[:]),
                               waits=env.now("ac") + ([("pe", hid_free_pe)] if kh == 0 else []), incs=[("dv", 1)])
                    hid_dv = env.cnt["dv"]
                    for kh in range(KH):
                        env.op("tensor", lambda e, kh=kh: e.matmul(out=pb[6][0:1, 0:G], lhsT=ones_bf[:, 0:1], rhs=hid[:, kh, :],
                                                                   start=(kh == 0), stop=(kh == KH - 1)),
                               waits=([("dv", hid_dv)] + env.now("ac") if kh == 0 else []), incs=[("pe", 1)])
                    env.op("scalar", lambda e: e.copy(out=sum_bf[:, G:2 * G], in_=pb[6][0:1, 0:G]), waits=env.now("pe"), incs=[("ac", 1)])
                    sh_ac = env.cnt["ac"]
                    for mg in range(4):
                        for kh in range(KH):
                            slot = kh % 6
                            mwd = env.op("gpsimd", lambda e, kh=kh, mg=mg, slot=slot: e.dma_start(out=wdt[slot].ap().rearrange("p a b -> p (a b)"),
                                                                                                 in_=wd8.ap()[kh][:, mg * 4:(mg + 1) * 4, :].rearrange("p a b -> p (a b)")),
                                         waits=dn_slot_free.get(slot, []), incs=[(f"qd{slot}", 16)])
                            for mt in range(4):
                                env.op("tensor", lambda e, kh=kh, mt=mt, slot=slot: e.matmul(out=pb[2 + mt][:, 0:G], lhsT=wdt[slot][:, mt, :], rhs=hid[:, kh, :],
                                                                                             start=(kh == 0), stop=False),
                                       waits=([(f"qd{slot}", mwd[f"qd{slot}"])] + ([("dv", env.cnt["dv"])] if kh == 0 else []) if mt == 0 else []), incs=[("pe", 1)])
                            dn_slot_free[slot] = [("pe", env.cnt["pe"])]
                        for mt in range(4):
                            env.op("tensor", lambda e, mt=mt: e.matmul(out=pb[2 + mt][:, 0:G], lhsT=neg128_bf[:], rhs=sum_bf[:, G:2 * G], start=False, stop=True),
                                   waits=([("ac", sh_ac)] if mt == 0 else []), incs=[("pe", 1)])
                        dpe = env.cnt["pe"]
                        for mt in range(4):
                            kd = mg * 4 + mt
                            env.op("vector", lambda e, mt=mt, kd=kd: e.scalar_tensor_tensor(out=x1g[:, kd, :], in0=pb[2 + mt][:, 0:G], scalar=sdh[:, kd:kd + 1],
                                                                                           in1=xg[:, kd, :], op0=ALU.mult, op1=ALU.add),
                                   waits=[("pe", dpe)] + (x1g_free if (mg == 0 and mt == 0) else []), incs=[("dv", 1)])
                    hid_free_pe = env.cnt["pe"]
                    x1_dv = env.cnt["dv"]
                    mst = env.op("sync", lambda e, g=g: e.dma_start(out=dst_dram.ap()[:, :, g * G:(g + 1) * G].rearrange("k p g -> p k g"),
                                                                    in_=x1g.ap()),
                                 waits=[("dv", x1_dv)], incs=[("qst", 16)])
                    if next_ln is not None:
                        nmu, nrstd, neps = next_ln
                        ln_stats(x1g, g, nmu, nrstd, neps, xb, xb_extra=[("dv", x1_dv)])
                    x1g_free = [("qst", mst["qst"])] + env.now("dv", "ac")
                    if do_halo and g == NG - 1:
                        env.op("vector", lambda e: e.tensor_scalar(out=halo_stg.ap().rearrange("p (k t) -> p k t", t=3),
                                                                   in0=x1g.ap()[:, :, G - 3:G],
                                                                   scalar1=masks[:, 1:2], scalar2=None, op0=ALU.mult),
                               waits=[("dv", x1_dv)], incs=[("dv", 1)])
                        mhs = env.op("sync", lambda e: e.dma_start(out=halo_snd.ap(), in_=halo_stg[:]), waits=env.now("dv"), incs=[("qh", 16)])
                        env.wait("gpsimd", [("qh", mhs["qh"])])
                        env.op("gpsimd", lambda e: e.collective_compute("AllReduce", ALU.add,
                                                                        replica_groups=[[0, 1], [2, 3], [4, 5], [6, 7]],
                                                                        ins=[halo_snd.ap().opt()], outs=[halo_rcv.ap().opt()]),
                               incs=[("cc", 1)])
                        env.wait("gpsimd", env.now("cc"))
                        env.op("gpsimd", lambda e: e.collective_compute("AllReduce", ALU.add,
                                                                        replica_groups=[[0, 1], [2, 3], [4, 5], [6, 7]],
                                                                        ins=[flag_snd.ap().opt()], outs=[flag_rcv.ap().opt()]),
                               incs=[("cc", 1)])
                env.barrier()

        p23 = ExitStack()
        psb = lambda name, shape, dtype: p23.enter_context(nc.sbuf_tensor(name, shape, dtype))
        b_conv = psb("b_conv", [64, TOK], BF16)
        c_conv = psb("c_conv", [64, TOK], BF16)
        lam_all = psb("lam_all", [128, NCH * 64], F32)

        def emit_p2():
            with ExitStack() as ps:
                s2 = lambda name, shape, dtype: ps.enter_context(nc.sbuf_tensor(name, shape, dtype))
                xc_full = s2("xc_full", [128, KD, TOK], BF16)
                rstd2_bc = s2("rstd2_bc", [128, TOK], F32)
                wit = [s2(f"wit{i}", [128, KD, 128], BF16) for i in range(2)]
                rawb = [s2(f"rawb{i}", [128, 3 + TOK], BF16) for i in range(2)]
                hconv = [s2(f"hconv{i}", [128, TOK], BF16) for i in range(2)]
                cacc = s2("cacc", [128, TOK], F32)
                tanh_s = s2("tanh_s", [128, TOK], BF16)
                dtp_fm = s2("dtp_fm", [64, TOK], BF16)
                dtv = s2("dtv", [64, G], F32)
                dtv2 = s2("dtv2", [64, G], F32)
                dtv3 = s2("dtv3", [64, G], F32)
                gstg = [s2(f"gstg{i}", [128, G], BF16) for i in range(2)]
                ystg = [s2(f"ystg{i}", [128, 1024], BF16) for i in range(2)]
                dtp_tm = s2("dtp_tm", [128, NCH * 64], BF16)
                ad_tm = s2("ad_tm", [128, NCH * 64], BF16)
                dec_all = s2("dec_all", [128, NCH * 64], BF16)
                ea_all = s2("ea_all", [64, NCH * 64], BF16)
                btm_all = s2("btm_all", [128, NCH * 64], BF16)
                gm_all = s2("gm_all", [128, NCH * 128], BF16)
                acs_sb = s2("acs_sb", [128, 64], F32)
                scr64 = s2("scr64", [128, 64], F32)
                hct = [s2(f"hct{i}", [128, 128], BF16) for i in range(2)]
                dh = [s2(f"dh{i}", [128, 128], BF16) for i in range(2)]
                segr = [s2(f"segr{i}", [128, 256], BF16) for i in range(2)]
                mt_sb = [s2(f"mt_sb{i}", [128, 256], BF16) for i in range(2)]
                ms_sb = [s2(f"ms_sb{i}", [64, 256], BF16) for i in range(2)]
                bdec = [s2(f"bdec{i}", [64, 128], BF16) if False else s2(f"bdec{i}", [128, 128], BF16) for i in range(2)]
                st_cur = s2("st_cur", [64, 128], BF16)
                st_tmp = s2("st_tmp", [64, 128], BF16)
                ytm = s2("ytm", [128, 128], BF16)
                st_stage = s2("st_stage", [64, 128], F32)
                halo_sbuf = s2("halo_sbuf", [128, KD * 3], F32)
                hb_bf = s2("hb_bf", [128, KD * 3], BF16)
                xch = s2("xch", [128, KD * 3], BF16)
                rstdh_bc = s2("rstdh_bc", [128, 3], F32)

                x1k = [s2(f"x1k{i}", [128, G], F32) for i in range(2)]
                x1k_free = {0: [], 1: []}
                for g in range(NG):
                    mb = env.op("tensor", lambda e, g=g: e.matmul(out=pb[7][:, 0:G], lhsT=ones_row(mu2_row), rhs=mu2_row[:, g * G:(g + 1) * G], start=True, stop=True),
                                waits=env.now("dv"), incs=[("pe", 1)])
                    for k in range(KD):
                        ks = k % 2
                        mx = env.op("sync", lambda e, g=g, k=k, ks=ks: e.dma_start(out=x1k[ks][:], in_=x1_sp.ap()[k, :, g * G:(g + 1) * G]),
                                    waits=x1k_free[ks], incs=[(f"qx{ks}", 16)])
                        env.op("vector", lambda e, g=g, k=k, ks=ks: e.tensor_tensor(out=xc_full[:, k, g * G:(g + 1) * G],
                                                                                    in0=x1k[ks][:], in1=pb[7][:, 0:G], op=ALU.subtract),
                               waits=[("pe", mb["pe"]), (f"qx{ks}", mx[f"qx{ks}"])], incs=[("dv", 1)])
                        x1k_free[ks] = [("dv", env.cnt["dv"])]
                    mb2 = env.op("tensor", lambda e, g=g: e.matmul(out=pb[6][:, 0:G], lhsT=ones_row(rstd2_row), rhs=rstd2_row[:, g * G:(g + 1) * G], start=True, stop=True),
                                 waits=env.now("dv"), incs=[("pe", 1)])
                    env.op("vector", lambda e, g=g: e.tensor_copy(out=rstd2_bc[:, g * G:(g + 1) * G], in_=pb[6][:, 0:G]),
                           waits=[("pe", mb2["pe"])], incs=[("dv", 1)])
                mhr = env.op("sync", lambda e: e.dma_start(out=halo_sbuf[:], in_=halo_rcv.ap()), waits=env.now("cc"), incs=[("qh", 16)])
                env.op("vector", lambda e: e.tensor_scalar(out=hb_bf[:], in0=halo_sbuf[:], scalar1=masks[:, 0:1], scalar2=None, op0=ALU.mult),
                       waits=[("qh", mhr["qh"])], incs=[("dv", 1)])
                hdv = env.cnt["dv"]
                hb3 = hb_bf.ap().rearrange("p (k t) -> p k t", t=3)
                for k in range(KD):
                    env.op("tensor", lambda e, k=k: e.matmul(out=pb[6][0:1, 0:3], lhsT=ones_bf[:, 0:1], rhs=hb3[:, k, :], start=(k == 0), stop=(k == KD - 1)),
                           waits=([("dv", hdv)] if k == 0 else []), incs=[("pe", 1)])
                e1 = env.op("scalar", lambda e: e.copy(out=hrow[:, 0:3], in_=pb[6][0:1, 0:3]), waits=env.now("pe"), incs=[("ac", 1)])
                mq = env.op("scalar", lambda e: e.activation(out=hb_bf[:], in_=hb_bf[:], func=AF.Square), waits=env.now("pe"), incs=[("ac", 1)])
                for k in range(KD):
                    env.op("tensor", lambda e, k=k: e.matmul(out=pb[6][0:1, 0:3], lhsT=ones_bf[:, 0:1], rhs=hb3[:, k, :], start=(k == 0), stop=(k == KD - 1)),
                           waits=([("ac", mq["ac"])] if k == 0 else []), incs=[("pe", 1)])
                env.op("scalar", lambda e: e.copy(out=hrow[:, 3:6], in_=pb[6][0:1, 0:3]), waits=env.now("pe"), incs=[("ac", 1)])
                env.op("vector", lambda e: e.tensor_scalar(out=hrow[:, 0:3], in0=hrow[:, 0:3], scalar1=1.0 / D, scalar2=None, op0=ALU.mult),
                       waits=env.now("ac"), incs=[("dv", 1)])
                env.op("vector", lambda e: e.tensor_scalar(out=hrow[:, 3:6], in0=hrow[:, 3:6], scalar1=1.0 / D, scalar2=None, op0=ALU.mult), incs=[("dv", 1)])
                env.op("scalar", lambda e: e.activation(out=hrow[:, 6:9], in_=hrow[:, 0:3], func=AF.Square), waits=env.now("dv"), incs=[("ac", 1)])
                env.op("vector", lambda e: e.tensor_sub(out=hrow[:, 9:12], in0=hrow[:, 3:6], in1=hrow[:, 6:9]), waits=env.now("ac"), incs=[("dv", 1)])
                env.op("scalar", lambda e: e.activation(out=hrow[:, 9:12], in_=hrow[:, 9:12], func=AF.Ln, bias=LN_EPS), waits=env.now("dv"), incs=[("ac", 1)])
                env.op("scalar", lambda e: e.activation(out=hrow[:, 12:15], in_=hrow[:, 9:12], func=AF.Exp, scale=-0.5), incs=[("ac", 1)])
                mbh = env.op("tensor", lambda e: e.matmul(out=pb[7][:, 0:3], lhsT=ones_f[0:1, :], rhs=hrow[:, 0:3], start=True, stop=True),
                             waits=env.now("ac", "dv"), incs=[("pe", 1)])
                env.op("vector", lambda e: e.scalar_tensor_tensor(out=xch.ap().rearrange("p (k t) -> p k t", t=3),
                                                                  in0=halo_sbuf.ap().rearrange("p (k t) -> p k t", t=3),
                                                                  scalar=masks[:, 0:1],
                                                                  in1=pb[7][:, 0:3].unsqueeze(1).broadcast_to((128, KD, 3)),
                                                                  op0=ALU.mult, op1=ALU.subtract),
                       waits=[("pe", mbh["pe"])], incs=[("dv", 1)])
                mbh2 = env.op("tensor", lambda e: e.matmul(out=pb[7][:, 0:3], lhsT=ones_f[0:1, :], rhs=hrow[:, 12:15], start=True, stop=True),
                              waits=env.now("dv"), incs=[("pe", 1)])
                env.op("vector", lambda e: e.tensor_copy(out=rstdh_bc[:], in_=pb[7][:, 0:3]), waits=[("pe", mbh2["pe"])], incs=[("dv", 1)])
                pre_dv = env.cnt["dv"]
                if dbg_halo is not None:
                    env.op("gpsimd", lambda e: e.dma_start(out=dbg_halo.ap()[:, 0:45], in_=xch[:, 0:45]), waits=env.now("dv"), incs=[("qdbg", 16)])
                    env.op("gpsimd", lambda e: e.dma_start(out=dbg_halo.ap()[:, 45:48], in_=rstdh_bc[:]), incs=[("qdbg", 16)])
                if dbg_xc is not None:
                    env.op("sync", lambda e: e.dma_start(out=dbg_xc.ap(), in_=xc_full.ap()), waits=env.now("dv"), incs=[("qdbg", 16)])

                wit_free = {0: [], 1: []}
                blk_psum_free = {0: pre_dv, 1: pre_dv, 2: pre_dv, 3: pre_dv, 4: pre_dv}

                def inproj_block(c0, w, bi, ragged):
                    slot = bi % 2
                    mw = env.op("gpsimd", lambda e, c0=c0, w=w, slot=slot: e.dma_start(out=wit[slot].ap()[:, :, 0:w],
                                                                                      in_=wiT.ap()[:, :, c0:c0 + w]),
                                waits=wit_free[slot], incs=[(f"qi{slot}", 16)])
                    for k in range(KD):
                        for g in range(NG):
                            env.op("tensor", lambda e, k=k, g=g, slot=slot, w=w: e.matmul(out=pb[g][0:w, 0:G], lhsT=wit[slot][:, k, 0:w],
                                                                                          rhs=xc_full[:, k, g * G:(g + 1) * G],
                                                                                          start=(k == 0), stop=(k == KD - 1)),
                                   waits=(([(f"qi{slot}", mw[f"qi{slot}"])] if g == 0 else []) + [("dv", blk_psum_free[g])] if k == 0 else []), incs=[("pe", 1)])
                        if ragged:
                            env.op("tensor", lambda e, k=k, slot=slot, w=w: e.matmul(out=pb[4][0:w, 0:3], lhsT=wit[slot][:, k, 0:w],
                                                                                     rhs=xch.ap().rearrange("p (kk t) -> p kk t", t=3)[:, k, :],
                                                                                     start=(k == 0), stop=(k == KD - 1)),
                                   waits=([("dv", pre_dv), ("dv", blk_psum_free[4])] if k == 0 else []), incs=[("pe", 1)])
                    wit_free[slot] = [("pe", env.cnt["pe"])]
                    return env.cnt["pe"]

                def conv_block(src_raw, dst_bf, jconv, raw_dv, dst_free=()):
                    env.op("vector", lambda e, j=jconv: e.tensor_scalar(out=cacc[:], in0=src_raw[:, 0:TOK], scalar1=convw[:, j * 4:j * 4 + 1],
                                                                        scalar2=convb[:, j:j + 1], op0=ALU.mult, op1=ALU.add),
                           waits=[("dv", raw_dv)], incs=[("dv", 1)])
                    for tap in range(1, 4):
                        env.op("vector", lambda e, j=jconv, tap=tap: e.scalar_tensor_tensor(out=cacc[:], in0=src_raw[:, tap:tap + TOK],
                                                                                            scalar=convw[:, j * 4 + tap:j * 4 + tap + 1],
                                                                                            in1=cacc[:], op0=ALU.mult, op1=ALU.add),
                               incs=[("dv", 1)])
                    cdv0 = env.cnt["dv"]
                    env.op("scalar", lambda e: e.activation(out=tanh_s[:], in_=cacc[:], func=AF.Tanh), waits=[("dv", cdv0)], incs=[("ac", 1)])
                    env.op("vector", lambda e: e.scalar_tensor_tensor(out=dst_bf[:], in0=tanh_s[:], scalar=1.0, in1=cacc[:],
                                                                      op0=ALU.add, op1=ALU.mult),
                           waits=env.now("ac") + list(dst_free), incs=[("dv", 1)])
                    return env.cnt["dv"]

                dpe = inproj_block(PROJ - 64, 64, 0, ragged=False)
                for g in range(NG):
                    env.op("vector", lambda e, g=g: e.tensor_tensor(out=dtv[:], in0=pb[g][0:64, 0:G], in1=rstd2_bc[0:64, g * G:(g + 1) * G], op=ALU.mult),
                           waits=[("pe", dpe)], incs=[("dv", 1)])
                    vdv = env.cnt["dv"]
                    env.op("scalar", lambda e: e.activation(out=dtv2[:], in_=dtv[:], func=AF.Relu, bias=dtb[:]), waits=[("dv", vdv)], incs=[("ac", 1)])
                    env.op("scalar", lambda e: e.activation(out=dtv3[:], in_=dtv[:], func=AF.Abs, bias=dtb[:]), incs=[("ac", 1)])
                    env.op("scalar", lambda e: e.activation(out=dtv[:], in_=dtv3[:], func=AF.Exp, scale=-1.0), incs=[("ac", 1)])
                    env.op("scalar", lambda e: e.activation(out=dtv3[:], in_=dtv[:], func=AF.Ln, bias=1.0), incs=[("ac", 1)])
                    env.op("vector", lambda e, g=g: e.tensor_add(out=dtp_fm[:, g * G:(g + 1) * G], in0=dtv2[:], in1=dtv3[:]),
                           waits=env.now("ac"), incs=[("dv", 1)])
                for g in range(NG):
                    blk_psum_free[g] = env.cnt["dv"]
                if dbg_dt is not None:
                    env.op("sync", lambda e: e.dma_start(out=dbg_dt.ap(), in_=dtp_fm[:]), waits=env.now("dv"), incs=[("qdbg", 16)])

                bpe = inproj_block(E + E, 128, 1, ragged=True)
                for g in range(NG):
                    env.op("vector", lambda e, g=g: e.tensor_tensor(out=rawb[0][:, 3 + g * G:3 + (g + 1) * G], in0=pb[g][:, 0:G],
                                                                    in1=rstd2_bc[:, g * G:(g + 1) * G], op=ALU.mult),
                           waits=[("pe", bpe)], incs=[("dv", 1)])
                env.op("vector", lambda e: e.tensor_tensor(out=rawb[0][:, 0:3], in0=pb[4][:, 0:3], in1=rstdh_bc[:], op=ALU.mult),
                       waits=[("pe", bpe)], incs=[("dv", 1)])
                rawdv = env.cnt["dv"]
                env.op("vector", lambda e: e.tensor_scalar(out=cacc[:], in0=rawb[0][:, 0:TOK], scalar1=convw[:, 32 * 4:32 * 4 + 1],
                                                           scalar2=convb[:, 32:33], op0=ALU.mult, op1=ALU.add),
                       waits=[("dv", rawdv)], incs=[("dv", 1)])
                for tap in range(1, 4):
                    env.op("vector", lambda e, tap=tap: e.scalar_tensor_tensor(out=cacc[:], in0=rawb[0][:, tap:tap + TOK],
                                                                               scalar=convw[:, 32 * 4 + tap:32 * 4 + tap + 1],
                                                                               in1=cacc[:], op0=ALU.mult, op1=ALU.add),
                           incs=[("dv", 1)])
                env.op("scalar", lambda e: e.activation(out=tanh_s[:], in_=cacc[:], func=AF.Tanh), waits=env.now("dv"), incs=[("ac", 1)])
                env.op("vector", lambda e: e.scalar_tensor_tensor(out=b_conv[:], in0=tanh_s[0:64, 0:TOK], scalar=1.0, in1=cacc[0:64, 0:TOK],
                                                                  op0=ALU.add, op1=ALU.mult),
                       waits=env.now("ac"), incs=[("dv", 1)])
                env.op("vector", lambda e: e.scalar_tensor_tensor(out=c_conv[:], in0=tanh_s[64:128, 0:TOK], scalar=1.0, in1=cacc[64:128, 0:TOK],
                                                                  op0=ALU.add, op1=ALU.mult),
                       incs=[("dv", 1)])
                bc_dv = env.cnt["dv"]
                for g in range(NG):
                    blk_psum_free[g] = env.cnt["dv"]
                blk_psum_free[4] = env.cnt["dv"]
                if dbg_bc is not None:
                    env.op("sync", lambda e: e.dma_start(out=dbg_bc.ap()[0:64, :], in_=b_conv[:]), waits=env.now("dv"), incs=[("qdbg", 16)])
                    env.op("sync", lambda e: e.dma_start(out=dbg_bc.ap()[64:128, :], in_=c_conv[:]), incs=[("qdbg", 16)])

                pp5_free = env.cnt["dv"]
                pb6_free = env.cnt["dv"]
                for c in range(NCH):
                    cc0 = c * 64
                    m1 = env.op("tensor", lambda e, c=c: e.transpose(out=pbb[5][:, 0:64], in_=dtp_fm[0:64, c * CS:(c + 1) * CS], identity=eye_bf[0:64, 0:64]),
                                waits=env.now("dv") + [("dv", pp5_free)], incs=[("pe", 1)])
                    env.op("vector", lambda e, cc0=cc0: e.tensor_copy(out=dtp_tm[:, cc0:cc0 + 64], in_=pbb[5][:, 0:64]),
                           waits=[("pe", m1["pe"])], incs=[("dv", 1)])
                    env.op("vector", lambda e, cc0=cc0: e.tensor_tensor(out=ad_tm[:, cc0:cc0 + 64], in0=pbb[5][:, 0:64], in1=a_bc[:], op=ALU.mult),
                           incs=[("dv", 1)])
                    ad_dv = env.cnt["dv"]
                    m2 = env.op("tensor", lambda e, cc0=cc0: e.matmul(out=pb[6][:, 0:64], lhsT=tri_bf[:], rhs=ad_tm[:, cc0:cc0 + 64], start=True, stop=True),
                                waits=[("dv", ad_dv), ("dv", pb6_free)], incs=[("pe", 1)])
                    m3 = env.op("tensor", lambda e, cc0=cc0: e.matmul(out=pb[6][0:1, 64:128], lhsT=ones_bf[:, 0:1], rhs=ad_tm[:, cc0:cc0 + 64], start=True, stop=True),
                                incs=[("pe", 1)])
                    env.op("vector", lambda e: e.tensor_copy(out=acs_sb[:, 0:64], in_=pb[6][:, 0:64]), waits=[("pe", m3["pe"])], incs=[("dv", 1)])
                    env.op("vector", lambda e: e.tensor_copy(out=ach_row[:], in_=pb[6][0:1, 64:128]), incs=[("dv", 1)])
                    acs_dv = env.cnt["dv"]
                    m4 = env.op("tensor", lambda e, c=c: e.transpose(out=pbb[5][:, 0:64], in_=b_conv[:, c * CS:(c + 1) * CS], identity=eye_bf[0:64, 0:64]),
                                waits=[("dv", acs_dv), ("dv", bc_dv)], incs=[("pe", 1)])
                    m5 = env.op("tensor", lambda e, c=c: e.matmul(out=pb[5][:, 64:192], lhsT=b_conv[:, c * CS:(c + 1) * CS],
                                                                  rhs=c_conv[:, c * CS:(c + 1) * CS], start=True, stop=True),
                                incs=[("pe", 1)])
                    env.op("vector", lambda e, cc0=cc0: e.tensor_copy(out=btm_all[:, cc0:cc0 + 64], in_=pbb[5][:, 0:64]),
                           waits=[("pe", m5["pe"])], incs=[("dv", 1)])
                    env.op("vector", lambda e, c=c: e.tensor_tensor(out=gm_all[:, c * 128:(c + 1) * 128], in0=pb[5][:, 64:192], in1=tri_f[:], op=ALU.mult),
                           incs=[("dv", 1)])
                    pp5_free = env.cnt["dv"]
                    env.op("scalar", lambda e: e.activation(out=ea_row[:], in_=ach_row[:], func=AF.Exp), waits=[("dv", acs_dv)], incs=[("ac", 1)])
                    ea_ac = env.cnt["ac"]
                    env.op("tensor", lambda e: e.matmul(out=pb[6][:, 128:192], lhsT=ones_f[0:1, :], rhs=runa_row[:, 0:64], start=True, stop=True),
                           waits=[("dv", acs_dv)], incs=[("pe", 1)])
                    env.op("tensor", lambda e: e.matmul(out=pb[6][:, 192:256], lhsT=ones_f[0:1, :], rhs=ach_row[:, 0:64], start=True, stop=True),
                           incs=[("pe", 1)])
                    m8 = env.op("tensor", lambda e: e.matmul(out=pb[6][0:64, 256:320], lhsT=ones_f[0:1, 0:64], rhs=ea_row[:, 0:64], start=True, stop=True),
                               waits=[("ac", ea_ac)], incs=[("pe", 1)])
                    env.op("vector", lambda e: e.tensor_add(out=scr64[:], in0=pb[6][:, 128:192], in1=acs_sb[:, 0:64]),
                           waits=[("pe", m8["pe"])], incs=[("dv", 1)])
                    env.op("scalar", lambda e, cc0=cc0: e.activation(out=lam_all[:, cc0:cc0 + 64], in_=scr64[:], func=AF.Exp),
                           waits=env.now("dv"), incs=[("ac", 1)])
                    env.op("vector", lambda e: e.tensor_sub(out=scr64[:], in0=pb[6][:, 192:256], in1=acs_sb[:, 0:64]),
                           waits=env.now("ac"), incs=[("dv", 1)])
                    env.op("scalar", lambda e, cc0=cc0: e.activation(out=dec_all[:, cc0:cc0 + 64], in_=scr64[:], func=AF.Exp),
                           waits=env.now("dv"), incs=[("ac", 1)])
                    env.op("vector", lambda e, cc0=cc0: e.tensor_copy(out=ea_all[:, cc0:cc0 + 64], in_=pb[6][0:64, 256:320]), incs=[("dv", 1)])
                    pb6_free = env.cnt["dv"]
                    env.op("vector", lambda e: e.tensor_add(out=runa_row[:, 0:64], in0=runa_row[:, 0:64], in1=ach_row[:, 0:64]),
                           waits=env.now("ac"), incs=[("dv", 1)])

                seg_free = env.cnt["dv"]
                y_free = env.cnt["dv"]
                b7_free = env.cnt["dv"]
                gstg_free = {0: [], 1: []}
                ystg_free = {0: [], 1: []}
                raw_free = {0: [("dv", bc_dv)], 1: []}
                hconv_free = {0: 0, 1: 0}
                ststg_free = []
                conv_dv = {}

                def gen_p2a(j):
                    slot = j % 2
                    mw = env.op("gpsimd", lambda e, j=j, slot=slot: e.dma_start(out=wit[slot].ap()[:, :, 0:128],
                                                                               in_=wiT.ap()[:, :, j * 128:(j + 1) * 128]),
                                waits=wit_free[slot], incs=[(f"qi{slot}", 16)])
                    yield
                    for k in range(KD):
                        for g in range(NG):
                            env.op("tensor", lambda e, k=k, g=g, slot=slot: e.matmul(out=pb[g][:, 0:G], lhsT=wit[slot][:, k, :],
                                                                                     rhs=xc_full[:, k, g * G:(g + 1) * G],
                                                                                     start=(k == 0), stop=(k == KD - 1)),
                                   waits=(([(f"qi{slot}", mw[f"qi{slot}"])] if g == 0 else []) + [("dv", blk_psum_free[g])] if k == 0 else []),
                                   incs=[("pe", 1)])
                        yield
                    gpe = env.cnt["pe"]
                    wit_free[slot] = [("pe", gpe)]
                    for g in range(NG):
                        env.op("vector", lambda e, g=g, slot=slot: e.scalar_tensor_tensor(out=gstg[slot][:], in0=pb[g][:, 0:G], scalar=0.5,
                                                                                          in1=rstd2_bc[:, g * G:(g + 1) * G], op0=ALU.mult, op1=ALU.mult),
                               waits=[("pe", gpe)] + (gstg_free[slot] if g == 0 else env.now(f"qg{slot}")), incs=[("dv", 1)])
                        env.op("scalar", lambda e, slot=slot: e.activation(out=tanh_s[:, 0:G], in_=gstg[slot][:], func=AF.Tanh),
                               waits=env.now("dv"), incs=[("ac", 1)])
                        env.op("vector", lambda e, slot=slot: e.scalar_tensor_tensor(out=gstg[slot][:], in0=tanh_s[:, 0:G], scalar=1.0,
                                                                                     in1=gstg[slot][:], op0=ALU.add, op1=ALU.mult),
                               waits=env.now("ac"), incs=[("dv", 1)])
                        env.op("sync", lambda e, g=g, j=j, slot=slot: e.dma_start(out=gate_sp.ap()[j][:, g * G:(g + 1) * G], in_=gstg[slot][:]),
                               waits=env.now("dv"), incs=[(f"qg{slot}", 16)])
                        blk_psum_free[g] = env.cnt["dv"]
                        yield
                    gstg_free[slot] = [(f"qg{slot}", env.cnt[f"qg{slot}"])]
                    mwh = env.op("gpsimd", lambda e, j=j, slot=slot: e.dma_start(out=wit[slot].ap()[:, :, 0:128],
                                                                                in_=wiT.ap()[:, :, E + j * 128:E + (j + 1) * 128]),
                                 waits=wit_free[slot], incs=[(f"qi{slot}", 16)])
                    yield
                    for k in range(KD):
                        for g in range(NG):
                            env.op("tensor", lambda e, k=k, g=g, slot=slot: e.matmul(out=pb[g][:, 0:G], lhsT=wit[slot][:, k, :],
                                                                                     rhs=xc_full[:, k, g * G:(g + 1) * G],
                                                                                     start=(k == 0), stop=(k == KD - 1)),
                                   waits=(([(f"qi{slot}", mwh[f"qi{slot}"])] if g == 0 else []) + [("dv", blk_psum_free[g])] if k == 0 else []),
                                   incs=[("pe", 1)])
                        env.op("tensor", lambda e, k=k, slot=slot: e.matmul(out=pb[4][:, 0:3], lhsT=wit[slot][:, k, :],
                                                                            rhs=xch.ap().rearrange("p (kk t) -> p kk t", t=3)[:, k, :],
                                                                            start=(k == 0), stop=(k == KD - 1)),
                               waits=([("dv", pre_dv), ("dv", blk_psum_free[4])] if k == 0 else []), incs=[("pe", 1)])
                        yield
                    hpe = env.cnt["pe"]
                    wit_free[slot] = [("pe", hpe)]
                    for g in range(NG):
                        env.op("vector", lambda e, g=g, slot=slot: e.tensor_tensor(out=rawb[slot][:, 3 + g * G:3 + (g + 1) * G], in0=pb[g][:, 0:G],
                                                                                   in1=rstd2_bc[:, g * G:(g + 1) * G], op=ALU.mult),
                               waits=[("pe", hpe)] + (raw_free[slot] if g == 0 else []), incs=[("dv", 1)])
                        yield
                    env.op("vector", lambda e, slot=slot: e.tensor_tensor(out=rawb[slot][:, 0:3], in0=pb[4][:, 0:3], in1=rstdh_bc[:], op=ALU.mult),
                           incs=[("dv", 1)])
                    for g in range(NG):
                        blk_psum_free[g] = env.cnt["dv"]
                    blk_psum_free[4] = env.cnt["dv"]
                    rdv = env.cnt["dv"]
                    env.op("vector", lambda e, j=j, slot=slot: e.tensor_scalar(out=cacc[:], in0=rawb[slot][:, 0:TOK], scalar1=convw[:, j * 4:j * 4 + 1],
                                                                               scalar2=convb[:, j:j + 1], op0=ALU.mult, op1=ALU.add),
                           waits=[("dv", rdv)], incs=[("dv", 1)])
                    yield
                    for tap in range(1, 4):
                        env.op("vector", lambda e, j=j, tap=tap, slot=slot: e.scalar_tensor_tensor(out=cacc[:], in0=rawb[slot][:, tap:tap + TOK],
                                                                                                   scalar=convw[:, j * 4 + tap:j * 4 + tap + 1],
                                                                                                   in1=cacc[:], op0=ALU.mult, op1=ALU.add),
                               incs=[("dv", 1)])
                        yield
                    cdv0 = env.cnt["dv"]
                    env.op("scalar", lambda e: e.activation(out=tanh_s[:], in_=cacc[:], func=AF.Tanh), waits=[("dv", cdv0)], incs=[("ac", 1)])
                    env.op("vector", lambda e, slot=slot: e.scalar_tensor_tensor(out=hconv[slot][:], in0=tanh_s[:], scalar=1.0, in1=cacc[:],
                                                                                 op0=ALU.add, op1=ALU.mult),
                           waits=env.now("ac") + [("pe", hconv_free[slot])], incs=[("dv", 1)])
                    conv_dv[j] = env.cnt["dv"]
                    raw_free[slot] = [("dv", conv_dv[j])]
                    yield

                def ssd_prep(j, c):
                    slot = j % 2
                    p = c % 2
                    cc0 = c * 64
                    h0 = 2 * j
                    cdv = conv_dv[j]
                    m1 = env.op("tensor", lambda e, c=c, slot=slot: e.transpose(out=pbb[7][:, 0:128], in_=hconv[slot][:, c * CS:(c + 1) * CS], identity=eye_bf[:]),
                                waits=[("dv", cdv), ("dv", self_free["b7"])], incs=[("pe", 1)])
                    env.op("vector", lambda e, cc0=cc0, h0=h0, p=p: e.tensor_tensor(out=hct[p].ap().rearrange("p (h q) -> p h q", h=2),
                                                                                   in0=pbb[7][:, 0:128].rearrange("p (h q) -> p h q", h=2),
                                                                                   in1=dtp_tm[:, cc0 + h0:cc0 + h0 + 2].unsqueeze(2).broadcast_to((128, 2, 64)),
                                                                                   op=ALU.mult),
                           waits=[("pe", m1["pe"])], incs=[("dv", 1)])
                    env.op("vector", lambda e, h0=h0, p=p: e.tensor_tensor(out=dh[p].ap().rearrange("p (h q) -> p h q", h=2),
                                                                           in0=pbb[7][:, 0:128].rearrange("p (h q) -> p h q", h=2),
                                                                           in1=d_bc[:, h0:h0 + 2].unsqueeze(2).broadcast_to((128, 2, 64)),
                                                                           op=ALU.mult),
                           incs=[("dv", 1)])
                    self_free["b7"] = env.cnt["dv"]
                    env.op("vector", lambda e, cc0=cc0, h0=h0, p=p: e.tensor_tensor(out=segr[p].ap().rearrange("p (h q) -> p h q", h=2),
                                                                                    in0=tri_bf.ap().unsqueeze(1).broadcast_to((128, 2, 128)),
                                                                                    in1=ad_tm[:, cc0 + h0:cc0 + h0 + 2].unsqueeze(2).broadcast_to((128, 2, 128)),
                                                                                    op=ALU.mult),
                           incs=[("dv", 1)])
                    sgdv = env.cnt["dv"]
                    for hh in range(2):
                        env.op("tensor", lambda e, hh=hh, p=p: e.matmul(out=pb[5][:, hh * 128:(hh + 1) * 128], lhsT=ut_bf[:],
                                                                        rhs=segr[p][:, hh * 128:(hh + 1) * 128], start=True, stop=True),
                               waits=([("dv", sgdv), ("ac", self_free["seg_ac"])] if hh == 0 else []), incs=[("pe", 1)])
                        env.op("tensor", lambda e, hh=hh, p=p: e.matmul(out=pb[5][0:64, 256 + hh * 128:256 + (hh + 1) * 128], lhsT=ones_bf[:, 0:64],
                                                                        rhs=segr[p][:, hh * 128:(hh + 1) * 128], start=True, stop=True),
                               incs=[("pe", 1)])
                    segpe = env.cnt["pe"]
                    env.op("scalar", lambda e, p=p: e.activation(out=mt_sb[p][:], in_=pb[5][:, 0:256], func=AF.Exp), waits=[("pe", segpe)], incs=[("ac", 1)])
                    env.op("scalar", lambda e, p=p: e.activation(out=ms_sb[p][:], in_=pb[5][0:64, 256:512], func=AF.Exp), incs=[("ac", 1)])
                    self_free["seg_ac"] = env.cnt["ac"]
                    exp_ac = env.cnt["ac"]
                    env.op("vector", lambda e, c=c, p=p: e.tensor_tensor(out=mt_sb[p].ap().rearrange("p (h q) -> p h q", h=2),
                                                                         in0=mt_sb[p].ap().rearrange("p (h q) -> p h q", h=2),
                                                                         in1=gm_all[:, c * 128:(c + 1) * 128].unsqueeze(1).broadcast_to((128, 2, 128)),
                                                                         op=ALU.mult),
                           waits=[("ac", exp_ac)], incs=[("dv", 1)])
                    env.op("vector", lambda e, c=c, p=p: e.tensor_tensor(out=ms_sb[p].ap().rearrange("p (h q) -> p h q", h=2),
                                                                         in0=ms_sb[p].ap().rearrange("p (h q) -> p h q", h=2),
                                                                         in1=c_conv[:, c * CS:(c + 1) * CS].unsqueeze(1).broadcast_to((64, 2, 128)),
                                                                         op=ALU.mult),
                           incs=[("dv", 1)])
                    env.op("vector", lambda e, cc0=cc0, h0=h0, p=p: e.tensor_tensor(out=bdec[p].ap().rearrange("p (h q) -> p h q", h=2),
                                                                                    in0=btm_all[:, cc0:cc0 + 64].unsqueeze(1).broadcast_to((128, 2, 64)),
                                                                                    in1=dec_all[:, cc0 + h0:cc0 + h0 + 2].unsqueeze(2).broadcast_to((128, 2, 64)),
                                                                                    op=ALU.mult),
                           incs=[("dv", 1)])
                    return env.cnt["dv"]

                def ssd_finish(j, c, prep_dv, st_ctx):
                    slot = j % 2
                    p = c % 2
                    cc0 = c * 64
                    h0 = 2 * j
                    for hh in range(2):
                        env.op("tensor", lambda e, hh=hh, p=p: e.matmul(out=pb[6][:, hh * 64:(hh + 1) * 64], lhsT=mt_sb[p][:, hh * 128:(hh + 1) * 128],
                                                                        rhs=hct[p][:, hh * 64:(hh + 1) * 64], start=True, stop=False),
                               waits=([("dv", prep_dv), ("dv", st_ctx["st_dv"]), ("dv", self_free["y"])] if hh == 0 else []), incs=[("pe", 1)])
                        env.op("tensor", lambda e, hh=hh, p=p: e.matmul(out=pb[6][:, hh * 64:(hh + 1) * 64], lhsT=ms_sb[p][0:64, hh * 128:(hh + 1) * 128],
                                                                        rhs=st_cur[0:64, hh * 64:(hh + 1) * 64], start=False, stop=True),
                               incs=[("pe", 1)])
                        env.op("tensor", lambda e, hh=hh, p=p: e.matmul(out=pb[6][0:64, 128 + hh * 64:192 + hh * 64], lhsT=bdec[p][:, hh * 64:(hh + 1) * 64],
                                                                        rhs=hct[p][:, hh * 64:(hh + 1) * 64], start=True, stop=True),
                               incs=[("pe", 1)])
                    ype = env.cnt["pe"]
                    env.op("vector", lambda e, cc0=cc0, h0=h0: e.tensor_tensor(out=st_tmp.ap().rearrange("p (h q) -> p h q", h=2),
                                                                               in0=st_cur.ap().rearrange("p (h q) -> p h q", h=2),
                                                                               in1=ea_all[0:64, cc0 + h0:cc0 + h0 + 2].unsqueeze(2).broadcast_to((64, 2, 64)),
                                                                               op=ALU.mult),
                           waits=[("pe", ype)], incs=[("dv", 1)])
                    env.op("vector", lambda e: e.tensor_add(out=st_cur[:], in0=st_tmp[:], in1=pb[6][0:64, 128:256]), incs=[("dv", 1)])
                    st_ctx["st_dv"] = env.cnt["dv"]
                    env.op("vector", lambda e, p=p: e.tensor_add(out=ytm[:], in0=pb[6][:, 0:128], in1=dh[p][:]), incs=[("dv", 1)])
                    ytm_dv = env.cnt["dv"]
                    m9 = env.op("tensor", lambda e: e.transpose(out=pbb[6][:, 512:640], in_=ytm[:], identity=eye_bf[:]),
                                waits=[("dv", ytm_dv)], incs=[("pe", 1)])
                    env.op("vector", lambda e, c=c, slot=slot: e.tensor_copy(out=ystg[slot][:, (c % 8) * CS:(c % 8 + 1) * CS], in_=pbb[6][:, 512:640]),
                           waits=[("pe", m9["pe"])] + (ystg_free[slot] if c == 0 else []), incs=[("dv", 1)])
                    self_free["y"] = env.cnt["dv"]
                    if c == NCH // 2 - 1 or c == NCH - 1:
                        half = 0 if c == NCH // 2 - 1 else 1
                        env.op("sync", lambda e, j=j, slot=slot, half=half: e.dma_start(out=y_sp.ap()[j][:, half * 1024:(half + 1) * 1024],
                                                                                       in_=ystg[slot][:]),
                               waits=env.now("dv"), incs=[(f"qy{slot}", 16)])

                def gen_ssd(j):
                    slot = j % 2
                    msz = env.op("vector", lambda e: e.memset(st_cur[:], 0.0), incs=[("dv", 1)])
                    st_ctx = {"st_dv": msz["dv"]}
                    prep_dv = ssd_prep(j, 0)
                    yield
                    for c in range(NCH):
                        nprep_dv = ssd_prep(j, c + 1) if c + 1 < NCH else None
                        ssd_finish(j, c, prep_dv, st_ctx)
                        prep_dv = nprep_dv
                        yield
                    hconv_free[slot] = env.cnt["pe"]
                    ystg_free[slot] = [(f"qy{slot}", env.cnt[f"qy{slot}"])]
                    env.op("vector", lambda e, j=j: e.tensor_scalar(out=st_stage[:], in0=st_cur[:], scalar1=masks[0:64, 1:2], scalar2=None, op0=ALU.mult),
                           waits=[("dv", st_ctx["st_dv"])] + ststg_free[0], incs=[("dv", 1)])
                    mstg = env.op("sync", lambda e, j=j: e.dma_start(out=st_snd.ap()[:, j * 128:(j + 1) * 128], in_=st_stage[:]),
                                  waits=env.now("dv"), incs=[("qss", 16)])
                    ststg_free[0] = [("qss", mstg["qss"])]
                    if dbg_st is not None:
                        env.op("sync", lambda e, j=j: e.dma_start(out=dbg_st.ap()[:, j * 128:(j + 1) * 128], in_=st_cur[:]),
                               waits=[("dv", st_ctx["st_dv"])], incs=[("qdbg", 16)])
                    yield

                self_free = {"seg_ac": env.cnt["ac"], "y": env.cnt["dv"], "b7": env.cnt["dv"]}
                ststg_free = [[]]
                for _ in gen_p2a(0):
                    pass
                for j in range(NBLK):
                    ssd = gen_ssd(j)
                    nxt = gen_p2a(j + 1) if j + 1 < NBLK else iter(())
                    for c in range(NCH + 1):
                        if next(ssd, None) is StopIteration:
                            pass
                        for _ in range(3):
                            next(nxt, None)
                    for _ in nxt:
                        pass
                    for _ in ssd:
                        pass
                env.wait("gpsimd", env.now("qss"))
                env.op("gpsimd", lambda e: e.collective_compute("AllReduce", ALU.add,
                                                                replica_groups=[[0, 1], [2, 3], [4, 5], [6, 7]],
                                                                ins=[st_snd.ap().opt()], outs=[st_rcv.ap().opt()]),
                       incs=[("cc", 1)])
                env.wait("gpsimd", env.now("cc"))
                env.op("gpsimd", lambda e: e.collective_compute("AllReduce", ALU.add,
                                                                replica_groups=[[0, 1], [2, 3], [4, 5], [6, 7]],
                                                                ins=[flag_snd.ap().opt()], outs=[flag_rcv.ap().opt()]),
                       incs=[("cc", 1)])
                env.barrier()

        def emit_p3():
            with ExitStack() as ps:
                s3 = lambda name, shape, dtype: ps.enter_context(nc.sbuf_tensor(name + "_p3", shape, dtype))
                xg = s3("xg", [128, KD, G], F32)
                xb = s3("xb", [128, KD, G], BF16)
                s_bf = s3("s_bf", [64, NH * P], BF16)
                yhg = s3("yhg", [128, NBLK, 256], BF16)
                ghg = s3("ghg", [128, NBLK, 256], BF16)
                yfn = s3("yfn", [128, NBLK, G], BF16)
                dytm = s3("dytm", [128, 128], BF16)
                ytm3 = s3("ytm3", [128, 128], BF16)
                x2g = s3("x2g", [128, KD, G], F32)
                wot = [s3(f"wot{i}", [128, 4, 128], BF16) for i in range(3)]

                msr = env.op("gpsimd", lambda e: e.dma_start(out=s_bf[:], in_=st_rcv.ap()), waits=env.now("cc"), incs=[("qsb", 16)])
                env.op("vector", lambda e: e.tensor_scalar(out=s_bf[:], in0=s_bf[:], scalar1=masks[0:64, 0:1], scalar2=None, op0=ALU.mult),
                       waits=[("qsb", msr["qsb"])], incs=[("dv", 1)])
                sin_dv = env.cnt["dv"]
                st = {"dy_free": sin_dv, "tp_free": sin_dv, "yfn_gate_pe": 0, "xg_free": []}
                wot_free = {0: [], 1: [], 2: []}

                def gen_dy(g):
                    for hg in range(2):
                        t0 = g * G + hg * 256
                        my = env.op("sync", lambda e, t0=t0: e.dma_start(out=yhg.ap(),
                                                                         in_=y_sp.ap()[:, :, t0:t0 + 256].rearrange("b p t -> p b t")),
                                    waits=env.now("dv"), incs=[(f"qp{hg}", 16)])
                        mg_ = env.op("sync", lambda e, t0=t0: e.dma_start(out=ghg.ap(),
                                                                          in_=gate_sp.ap()[:, :, t0:t0 + 256].rearrange("b p t -> p b t")),
                                     incs=[(f"qp{hg}", 16)])
                        ldw = [(f"qp{hg}", mg_[f"qp{hg}"])]
                        yield
                        for bi in range(NBLK):
                            for ci in range(2):
                                c = (t0 + ci * CS) // CS
                                cc0 = c * 64
                                for hh in range(2):
                                    env.op("tensor", lambda e, bi=bi, c=c, hh=hh: e.matmul(out=pb[4][:, hh * 64:(hh + 1) * 64],
                                                                                           lhsT=c_conv[:, c * CS:(c + 1) * CS],
                                                                                           rhs=s_bf[0:64, (2 * bi + hh) * 64:(2 * bi + hh + 1) * 64],
                                                                                           start=True, stop=True),
                                           waits=([("dv", st["dy_free"])] + ldw if hh == 0 else []), incs=[("pe", 1)])
                                dype = env.cnt["pe"]
                                for hh in range(2):
                                    env.op("vector", lambda e, bi=bi, cc0=cc0, hh=hh: e.tensor_scalar(out=dytm[:, hh * 64:(hh + 1) * 64],
                                                                                                      in0=pb[4][:, hh * 64:(hh + 1) * 64],
                                                                                                      scalar1=lam_all[:, cc0 + 2 * bi + hh:cc0 + 2 * bi + hh + 1],
                                                                                                      scalar2=None, op0=ALU.mult),
                                           waits=([("pe", dype)] if hh == 0 else []), incs=[("dv", 1)])
                                st["dy_free"] = env.cnt["dv"]
                                m9 = env.op("tensor", lambda e: e.transpose(out=pbb[5][:, 0:128], in_=dytm[:], identity=eye_bf[:]),
                                            waits=[("dv", st["dy_free"]), ("dv", st["tp_free"])], incs=[("pe", 1)])
                                env.op("vector", lambda e, bi=bi, ci=ci: e.tensor_add(out=ytm3[:], in0=pbb[5][:, 0:128], in1=yhg[:, bi, ci * CS:(ci + 1) * CS]),
                                       waits=[("pe", m9["pe"])], incs=[("dv", 1)])
                                st["tp_free"] = env.cnt["dv"]
                                env.op("vector", lambda e, bi=bi, ci=ci: e.tensor_mul(out=ghg[:, bi, ci * CS:(ci + 1) * CS], in0=ytm3[:], in1=ghg[:, bi, ci * CS:(ci + 1) * CS]),
                                       incs=[("dv", 1)])
                            if bi % 4 == 3:
                                yield
                        yf_dv = env.cnt["dv"]
                        env.op("scalar", lambda e: e.activation(out=yhg.ap().rearrange("p b t -> p (b t)"), in_=ghg.ap().rearrange("p b t -> p (b t)"),
                                                                func=AF.Square), waits=[("dv", yf_dv)], incs=[("ac", 1)])
                        sq_ac = env.cnt["ac"]
                        yield
                        for bi in range(NBLK):
                            env.op("tensor", lambda e, bi=bi: e.matmul(out=pb[6][0:1, 0:256], lhsT=ones_bf[:, 0:1], rhs=yhg[:, bi, :],
                                                                       start=(bi == 0), stop=(bi == NBLK - 1)),
                                   waits=([("ac", sq_ac)] if bi == 0 else []), incs=[("pe", 1)])
                        env.op("scalar", lambda e: e.copy(out=rms_row[:, 0:256], in_=pb[6][0:1, 0:256]), waits=env.now("pe"), incs=[("ac", 1)])
                        env.op("vector", lambda e: e.tensor_scalar(out=rms_row[:, 0:256], in0=rms_row[:, 0:256], scalar1=1.0 / E, scalar2=None, op0=ALU.mult),
                               waits=env.now("ac"), incs=[("dv", 1)])
                        env.op("scalar", lambda e: e.activation(out=rms_row[:, 0:256], in_=rms_row[:, 0:256], func=AF.Ln, bias=RMS_EPS),
                               waits=env.now("dv"), incs=[("ac", 1)])
                        env.op("scalar", lambda e: e.activation(out=rms_row[:, 256:512], in_=rms_row[:, 0:256], func=AF.Exp, scale=-0.5), incs=[("ac", 1)])
                        mrb = env.op("tensor", lambda e: e.matmul(out=pb[7][:, 0:256], lhsT=ones_f[0:1, :], rhs=rms_row[:, 256:512], start=True, stop=True),
                                     waits=env.now("ac", "dv"), incs=[("pe", 1)])
                        yield "yfn"
                        for bi in range(NBLK):
                            env.op("vector", lambda e, bi=bi, hg=hg: e.scalar_tensor_tensor(out=yfn[:, bi, hg * 256:(hg + 1) * 256], in0=ghg[:, bi, :],
                                                                                            scalar=nw[:, bi:bi + 1], in1=pb[7][:, 0:256],
                                                                                            op0=ALU.mult, op1=ALU.mult),
                                   waits=([("pe", mrb["pe"]), ("pe", st["yfn_gate_pe"])] if bi == 0 else []), incs=[("dv", 1)])
                        yield
                    env.op("sync", lambda e, g=g: e.dma_start(out=xg.ap(),
                                                              in_=x1_sp.ap()[:, :, g * G:(g + 1) * G].rearrange("k p g -> p k g")),
                           waits=env.now("ac") + st["xg_free"], incs=[("qx0", 16)])
                    st["xg_qx"] = env.cnt["qx0"]
                    yield

                def emit_outproj(g, nxt, nxt_state):
                    yfn_dv = env.cnt["dv"]
                    op_free = env.cnt["dv"]
                    for mq in range(4):
                        for f in range(32):
                            slot = f % 3
                            mw = env.op("gpsimd", lambda e, f=f, mq=mq, slot=slot: e.dma_start(out=wot[slot].ap().rearrange("p a b -> p (a b)"),
                                                                                              in_=woT.ap()[f][:, mq * 4:(mq + 1) * 4, :].rearrange("p a b -> p (a b)")),
                                         waits=wot_free[slot], incs=[(f"qo{slot}", 16)])
                            for m4 in range(4):
                                env.op("tensor", lambda e, f=f, m4=m4, slot=slot: e.matmul(out=pb[m4][:, 0:G], lhsT=wot[slot][:, m4, :], rhs=yfn[:, f, :],
                                                                                           start=(f == 0), stop=(f == 31)),
                                       waits=(([(f"qo{slot}", mw[f"qo{slot}"]), ("dv", yfn_dv), ("dv", op_free), ("qx0", st.get("xg_qx", 0))]) if m4 == 0 and f == 0 else
                                              ([(f"qo{slot}", mw[f"qo{slot}"])] if m4 == 0 else [])), incs=[("pe", 1)])
                            wot_free[slot] = [("pe", env.cnt["pe"])]
                            if f % 8 == 7 and not nxt_state["blocked"]:
                                if next(nxt, None) == "yfn":
                                    nxt_state["blocked"] = True
                        oppe = env.cnt["pe"]
                        for m4 in range(4):
                            kd = mq * 4 + m4
                            env.op("vector", lambda e, m4=m4, kd=kd: e.tensor_add(out=x2g[:, kd, :], in0=pb[m4][:, 0:G], in1=xg[:, kd, :]),
                                   waits=[("pe", oppe)], incs=[("dv", 1)])
                        op_free = env.cnt["dv"]
                    st["yfn_gate_pe"] = env.cnt["pe"]
                    x2_dv = env.cnt["dv"]
                    st["xg_free"] = [("dv", x2_dv)]
                    env.op("sync", lambda e, g=g: e.dma_start(out=x2_sp.ap()[:, :, g * G:(g + 1) * G].rearrange("k p g -> p k g"),
                                                              in_=x2g.ap()),
                           waits=[("dv", x2_dv)], incs=[("qst", 16)])
                    ln_stats(x2g, g, mu3_row, rstd3_row, LN_EPS, xb, xb_extra=[("dv", x2_dv)])

                g0 = gen_dy(0)
                for _ in g0:
                    pass
                for g in range(NG):
                    nxt = gen_dy(g + 1) if g + 1 < NG else iter(())
                    nxt_state = {"blocked": False}
                    emit_outproj(g, nxt, nxt_state)
                    for _ in nxt:
                        pass
                env.barrier()

        emit_ffn(x_in, x1_sp, None, (mu2_row, rstd2_row, LN_EPS), first_wait=prep_done, do_halo=(upto >= 2))
        if upto >= 2:
            emit_p2()
        if upto >= 3:
            emit_p3()
        p23.close()
        if upto >= 4:
            emit_ffn(x2_sp, out_d, (mu3_row, rstd3_row), None, first_wait=env.now("dv", "ac", "qs", "pe"))
        env.wait("sync", [(n, env.cnt[n]) for n in env.sems if n.startswith("q")])
        env.wait("gpsimd", [(n, env.cnt[n]) for n in env.sems if n.startswith("q")])
        env.replay(ctx.enter_context(nc.Block()))
    return nc


def _prep_inputs(inputs):
    x = np.asarray(inputs["L0_attn_x"])
    wi = np.asarray(inputs["L0_attn_in_proj_w"])
    wo = np.asarray(inputs["L0_attn_out_proj_w"])
    cw = np.asarray(inputs["L0_attn_conv_w"])
    cb = np.asarray(inputs["L0_attn_conv_b"])
    dtbv = np.asarray(inputs["L0_attn_dt_bias"])
    alog = np.asarray(inputs["L0_attn_A_log"])
    dd = np.asarray(inputs["L0_attn_D"])
    nwv = np.asarray(inputs["L0_attn_norm_w"])
    wqg = np.asarray(inputs["L0_ffn_wq_gate"])
    sgv = np.asarray(inputs["L0_ffn_s_gate"])
    wqu = np.asarray(inputs["L0_ffn_wq_up"])
    suv = np.asarray(inputs["L0_ffn_s_up"])
    wqd = np.asarray(inputs["L0_ffn_wq_down"])
    sdv = np.asarray(inputs["L0_ffn_s_down"])

    def u8(a):
        return np.ascontiguousarray(np.ascontiguousarray(a).view("<u1").reshape(*a.shape, 4)[..., 0])

    wiT = np.ascontiguousarray(np.ascontiguousarray(wi.T).reshape(KD, 128, PROJ).transpose(1, 0, 2))
    woT = np.ascontiguousarray(np.ascontiguousarray(wo.T).reshape(32, 128, KD, 128))
    wg8a = np.ascontiguousarray(u8(wqg).T.reshape(KD, 128, KH, 128).transpose(2, 1, 0, 3))
    wu8a = np.ascontiguousarray(u8(wqu).T.reshape(KD, 128, KH, 128).transpose(2, 1, 0, 3))
    wd8a = np.ascontiguousarray(u8(wqd).T.reshape(KH, 128, KD, 128))

    consts = np.zeros((6, 128, 128), np.float32)
    consts[0] = np.eye(128, dtype=np.float32)
    idx = np.arange(128)
    consts[1] = (idx[:, None] <= idx[None, :]).astype(np.float32)
    consts[2] = (idx[:, None] > idx[None, :]).astype(np.float32)
    consts[3] = 1.0
    consts[4] = -128.0
    consts[5] = 0.5

    common = {
        "wiT": wiT, "woT": woT, "wg8": wg8a, "wu8": wu8a, "wd8": wd8a, "consts": consts,
        "convw": np.ascontiguousarray(cw.reshape(33, 128, KCONV).transpose(1, 0, 2).reshape(128, 33 * KCONV)),
        "convb": np.ascontiguousarray(cb.reshape(33, 128).T),
        "dtb": np.ascontiguousarray(dtbv.reshape(64, 1)),
        "alog": np.ascontiguousarray(alog.reshape(64, 1)),
        "drow": np.ascontiguousarray(dd.reshape(1, 64)),
        "nw": np.ascontiguousarray(nwv.reshape(32, 128).T),
        "sg": np.ascontiguousarray(sgv.reshape(KH, 128).T),
        "su": np.ascontiguousarray(suv.reshape(KH, 128).T),
        "sd": np.ascontiguousarray(sdv.reshape(KD, 128).T),
    }
    in_maps = []
    for c in range(8):
        b, h = c // 2, c % 2
        xs = np.ascontiguousarray(np.ascontiguousarray(x[b, h * TOK:(h + 1) * TOK, :].T).reshape(KD, 128, TOK))
        m = dict(common)
        m["x_in"] = xs
        m["fconst"] = np.tile(np.array([[1e-5, 1e-6, 1.0, -0.5, -1.0, 0.5, 0.0, 2.0]], np.float32), (128, 1))
        msk = np.zeros((128, 2), np.float32)
        msk[:, 0] = 1.0 if h == 1 else 0.0
        msk[:, 1] = 1.0 if h == 0 else 0.0
        m["masks"] = msk
        in_maps.append(m)
    return in_maps


def kernel(**inputs):
    in_maps = _prep_inputs(inputs)
    nc = build_program()
    res = run_bass_kernel_spmd(nc, in_maps, core_ids=list(range(8)))
    out = np.empty((B, S, D), np.float32)
    for c in range(8):
        b, h = c // 2, c % 2
        o = res.results[c]["out"]
        out[b, h * TOK:(h + 1) * TOK, :] = o.reshape(D, TOK).T
    return out
